# version 08
# baseline (speedup 1.0000x reference)
import copy

import numpy as np
from contextlib import ExitStack

import concourse.bass as bass
import concourse.mybir as mybir
from concourse.bass_utils import run_bass_kernel_spmd

F32 = mybir.dt.float32
BF16 = mybir.dt.bfloat16
AF = mybir.ActivationFunctionType
OP = mybir.AluOpType
AX = mybir.AxisListType.X

D = 1024
S = 4096
HALF = 2048
NT = 16
NKT = 32
NH = 16
DH = 64
KC = 32
FF = 4096
NE = 8
GI = 2048
EPS = 1e-5
NQ = 4

PHASES = 4
A_STEPS = 6
KV_STEPS = 4


class Ctr:
    __slots__ = ("sem", "n", "step")

    def __init__(self, sem, step=1):
        self.sem, self.n, self.step = sem, 0, step


LANES = {"LX": ["LX0", "LX1", "LX2", "LX3"],
         "LW": ["LW0", "LW1", "LW2", "LW3", "LW4", "LW5"],
         "STO": ["SO0", "SO1"]}


class Em:
    def __init__(self, k, eng, C):
        self.k, self.e, self.C = k, eng, C

    def dma(self, pool, key, f):
        idx = self.C["_pool_" + pool]
        names = LANES[pool]
        name = names[idx.n % len(names)]
        idx.n += 1
        c = self.C[name]
        self.w(key, name, c.n)
        c.n += 16
        tok = (name, c.n)
        if self.k == key:
            f(self.e).then_inc(c.sem, 16)
        return tok

    def wt(self, key, tok):
        if tok and self.k == key:
            self.e.wait_ge(self.C[tok[0]].sem, tok[1])

    def op(self, key, f, inc=None):
        v = None
        if inc is not None:
            c = self.C[inc]
            c.n += c.step
            v = c.n
        if self.k == key:
            ins = f(self.e)
            if inc is not None:
                ins.then_inc(self.C[inc].sem, self.C[inc].step)
        return v

    def w(self, key, cname, val):
        if self.k == key and val and val > 0:
            self.e.wait_ge(self.C[cname].sem, val)

    def t(self, f, inc=None):
        return self.op("t", f, inc)

    def v(self, f, inc=None):
        return self.op("v", f, inc)

    def s(self, f, inc=None):
        return self.op("s", f, inc)

    def g(self, f, inc=None):
        return self.op("g", f, inc)

    def y(self, f, inc=None):
        return self.op("y", f, inc)


def re_src(src2d, p=128):
    return src2d.rearrange("(t p) d -> p t d", p=p)


def re_dst(dst2d, cols):
    return dst2d.rearrange("p (t d) -> p t d", d=cols)


def ln_tile(E, sh, src_ap, dst_ap, slot, gain=None, bias=None, ln_gate=None,
            pre_v_wait=0, pre_wait=None):
    stat, lnt = sh["stat"], sh["lnt"]
    so = 16 * slot
    if ln_gate is not None and ln_gate.get(slot, 0):
        E.w("v", "SS", ln_gate[slot])
    if pre_v_wait:
        E.w("v", "VV", pre_v_wait)
    if pre_wait:
        E.w("v", pre_wait[0], pre_wait[1])
    E.v(lambda e: e.bn_stats(stat[:, so:so + 6], src_ap[:, 0:512]))
    v0 = E.v(lambda e: e.bn_stats(stat[:, so + 6:so + 12], src_ap[:, 512:1024]), "VV")
    E.w("v", "VV", v0)
    v1 = E.v(lambda e: e.bn_aggr(stat[:, so + 12:so + 14], stat[:, so:so + 12]), "VV")
    E.w("v", "VV", v1)
    E.v(lambda e: e.tensor_scalar_mul(stat[:, so + 14:so + 15], stat[:, so + 12:so + 13], -1.0))
    v2 = E.v(lambda e: e.tensor_scalar_add(stat[:, so + 15:so + 16], stat[:, so + 13:so + 14], EPS), "VV")
    E.w("s", "VV", v2)
    s0 = E.s(lambda e: e.activation(out=stat[:, so + 13:so + 14], in_=stat[:, so + 15:so + 16], func=AF.Ln), "SS")
    E.w("s", "SS", s0)
    s1 = E.s(lambda e: e.activation(out=stat[:, so + 15:so + 16], in_=stat[:, so + 13:so + 14],
                                    func=AF.Exp, scale=-0.5), "SS")
    if ln_gate is not None:
        ln_gate[slot] = s1
    E.w("v", "SS", s1)
    if gain is None:
        return E.v(lambda e: e.tensor_scalar(out=dst_ap, in0=src_ap,
                                             scalar1=stat[:, so + 14:so + 15],
                                             scalar2=stat[:, so + 15:so + 16],
                                             op0=OP.add, op1=OP.mult), "VV")
    v3 = E.v(lambda e: e.tensor_scalar(out=lnt[:, 0:D], in0=src_ap,
                                       scalar1=stat[:, so + 14:so + 15],
                                       scalar2=stat[:, so + 15:so + 16],
                                       op0=OP.add, op1=OP.mult), "VV")
    E.w("v", "VV", v3)
    v4 = E.v(lambda e: e.tensor_mul(lnt[:, 0:D], lnt[:, 0:D], gain), "VV")
    E.w("v", "VV", v4)
    return E.v(lambda e: e.tensor_add(dst_ap, lnt[:, 0:D], bias), "VV")


def phase_gate(E, gate, keys=("g", "y")):
    for key in keys:
        for k2, v in gate.items():
            E.w(key, k2, v)


def bcast_vec(E, sh, dram_row_ap, dst_sb, pb):
    C = sh["C"]
    ones, stg1 = sh["ones"], sh["stg1"]
    E.w("y", "TT", sh.get("stg1_tt", 0))
    lx = E.dma("LX", "y", lambda e: e.dma_start(out=stg1[0:1, 0:D], in_=dram_row_ap))
    E.wt("t", lx)
    tv = 0
    for j in range(2):
        tv = E.t(lambda e, j=j: e.matmul(pb[:, j * 512:(j + 1) * 512],
                                         ones[0:1, 0:128], stg1[0:1, j * 512:(j + 1) * 512],
                                         start=True, stop=True, skip_group_check=True), "TT")
    sh["stg1_tt"] = tv
    E.w("s", "TT", tv)
    return E.s(lambda e: e.copy(out=dst_sb[:, 0:D], in_=pb[:, 0:D]), "SS")


def rank_counts(E, sh, dd, lgt_psum):
    E.v(lambda e: e.memset(dd["cA"][:], 0.0))
    E.v(lambda e: e.memset(dd["cB"][:], 0.0))
    vc = E.v(lambda e: e.tensor_copy(out=dd["lgt"][:], in_=lgt_psum), "VV")
    E.w("v", "VV", vc)
    lgt_ap = dd["lgt"][:]
    for d_ in range(1, NE):
        E.v(lambda e, d_=d_: e.tensor_tensor(
            out=dd["sA"][:, 0:NE - d_], in0=lgt_ap[:, d_:NE],
            in1=lgt_ap[:, 0:NE - d_], op=OP.is_gt))
        v1 = E.v(lambda e, d_=d_: e.tensor_tensor(
            out=dd["sB"][:, d_:NE], in0=lgt_ap[:, 0:NE - d_],
            in1=lgt_ap[:, d_:NE], op=OP.is_ge), "VV")
        E.w("v", "VV", v1)
        E.v(lambda e, d_=d_: e.tensor_add(dd["cA"][:, 0:NE - d_],
                                          dd["cA"][:, 0:NE - d_], dd["sA"][:, 0:NE - d_]))
        v2 = E.v(lambda e, d_=d_: e.tensor_add(dd["cB"][:, d_:NE],
                                               dd["cB"][:, d_:NE], dd["sB"][:, d_:NE]), "VV")
        E.w("v", "VV", v2)
    vf = E.v(lambda e: e.tensor_add(dd["cA"][:], dd["cA"][:], dd["cB"][:]), "VV")
    E.w("v", "VV", vf)
    return vf


def build_nc():
    nc = bass.Bass("TRN2", target_bir_lowering=False)
    io = {}

    def din(name, shape, dt=F32):
        io[name] = nc.dram_tensor(name, shape, dt, kind="ExternalInput").ap()

    din("xb", [S, D])
    din("rowflag", [128, 1])
    din("lnv", [6, D])
    din("wq", [128, 8 * D]); din("wkv", [128, 8 * 128]); din("wo", [64, NH * D])
    din("gwT", [128, 8 * NE]); din("gbv", [1, NE])
    din("wib", [128, 32]); din("wob", [1, D])
    din("wiT", [8, 128, 8 * 512]); din("woT", [8, 128, 4 * D])
    din("laT", [128, 8 * 32]); din("lbT", [32, FF])
    din("w1", [5, 128, 8 * D])
    din("rwT", [128, 8 * NE]); din("rbv", [1, NE])
    din("guT", [NE, 16, 128, 2048]); din("gub", [NE, 128, 32])
    din("dnT", [NE, 128, 16 * D]); din("dnb", [1, NE * D])
    io["out"] = nc.dram_tensor("out", [HALF, D], F32, kind="ExternalOutput").ap()

    with ExitStack() as STK:
        sb = lambda n_, s_, d_: STK.enter_context(nc.sbuf_tensor(n_, s_, d_))
        ps = lambda n_, s_, d_: STK.enter_context(nc.psum_tensor(n_, s_, d_))

        resid = sb("resid", [128, NT * D], F32)
        idf = sb("idf", [128, 128], F32)
        idb = sb("idb", [128, 128], BF16)
        ones = sb("ones", [128, 128], F32)
        ones_bf = sb("ones_bf", [1, 512], BF16)
        cvals = sb("cvals", [128, 2], F32)
        rflag = sb("rflag", [128, 1], F32)
        stat = sb("stat", [128, 32], F32)
        lnt = sb("lnt", [128, D], F32)
        stg1 = sb("stg1", [1, D], F32)

        pA = [ps(f"pA{i}", [128, 512], F32) for i in range(3)]
        pB = [ps(f"pB{i}", [128, 1024], F32) for i in range(2)]
        pF = ps("pF", [128, 1024], BF16)

        semnames = (["TT", "VV", "SS", "GG"] + LANES["LX"] + LANES["LW"]
                    + LANES["STO"])
        sems = {k: STK.enter_context(nc.semaphore(k)) for k in semnames}
        C = {k: Ctr(sems[k], 16 if k[:2] in ("LX", "LW", "SO") else 1)
             for k in semnames}
        for pool in LANES:
            C["_pool_" + pool] = Ctr(None, 1)

        nc.const_aps.register(F32, 7.0, cvals[:, 0:1])
        nc.const_aps.register(F32, 1.702 * 7.0, cvals[:, 1:2])

        block = STK.enter_context(nc.Block())
        ENGS = [("y", "sync"), ("g", "gpsimd"), ("t", "tensor"),
                ("v", "vector"), ("s", "scalar")]

        def run_phase(body, state=None):
            snap = {k2: c.n for k2, c in C.items()}
            st_snap = copy.deepcopy(state) if state is not None else None
            finals = None
            fin_state = None
            for idx, (key, attr) in enumerate(ENGS):
                for k2, c in C.items():
                    c.n = snap[k2]
                if state is not None and idx > 0:
                    fin_state = copy.deepcopy(state)
                    state.clear()
                    state.update(copy.deepcopy(st_snap))
                def f(eng, key=key):
                    body(Em(key, eng, C))
                getattr(block, attr)(f)
                cur = {k2: c.n for k2, c in C.items()}
                if finals is None:
                    finals = cur
                else:
                    assert cur == finals, f"counter divergence {key}: {cur} vs {finals}"
                if state is not None and fin_state is not None:
                    assert state == fin_state, f"state divergence on {key}"

        sh = dict(nc=nc, io=io, C=C, resid=resid, idf=idf, idb=idb, ones=ones,
                  ones_bf=ones_bf, cvals=cvals, rflag=rflag,
                  stat=stat, lnt=lnt, stg1=stg1, pA=pA, pB=pB, pF=pF)

        def init_body(E):
            E.v(lambda e: e.memset(ones[:], 1.0))
            E.v(lambda e: e.memset(ones_bf[:], 1.0))
            E.v(lambda e: e.memset(cvals[:, 0:1], 7.0))
            E.v(lambda e: e.memset(cvals[:, 1:2], 1.702 * 7.0), "VV")
            g1 = E.g(lambda e: e.memset(idf[:], 1.0), "GG")
            E.w("g", "GG", g1)
            E.g(lambda e: e.affine_select(out=idf[:], in_=idf[:], pattern=[[1, 128]],
                                          base=0, channel_multiplier=-1,
                                          compare_op=OP.is_equal, fill=0.0))
            g2 = E.g(lambda e: e.memset(idb[:], 1.0), "GG")
            E.w("g", "GG", g2)
            E.g(lambda e: e.affine_select(out=idb[:], in_=idb[:], pattern=[[1, 128]],
                                          base=0, channel_multiplier=-1,
                                          compare_op=OP.is_equal, fill=0.0), "GG")
            tok = E.dma("LX", "y", lambda e: e.dma_start(out=rflag[:], in_=io["rowflag"][:]))
            init_body.tok = tok
            for key in ("t", "v", "s", "g", "y"):
                E.w(key, "VV", C["VV"].n)
                E.w(key, "GG", C["GG"].n)
        run_phase(init_body)
        init_lx = ("LX0", 16)

        with ExitStack() as SA:
            sba = lambda n_, s_, d_: SA.enter_context(nc.sbuf_tensor(n_, s_, d_))
            A = dict(
                gbc=sba("gbcA", [128, D], F32),
                bbc=sba("bbcA", [128, D], F32),
                wbuf=sba("wbuf", [128, NH * D], BF16),
                wkv_sb=sba("wkv_sb", [128, 8 * 128], BF16),
                kT=sba("kT", [128, S], BF16),
                vones=sba("vones", [128, KC * 128], BF16),
                qhT=sba("qhT", [128, NH * 512], BF16),
                oaT=sba("oaT", [128, NH * 512], BF16),
                gbc2=sba("gbc2", [128, D], F32),
                bbc2=sba("bbc2", [128, D], F32),
                xst=sba("xst", [128, 2 * D], F32),
                xnb=sba("xnb", [128, 2 * D], BF16),
                xnT=sba("xnT", [128, 2 * D], BF16),
                qtm=sba("qtm", [128, 2 * D], BF16),
                q2a=sba("q2a", [128, NQ * NH], F32),
                sqs=sba("sqs", [128, D], F32),
                pTs=sba("pTs", [128, 3 * 1024], BF16),
                und=sba("und", [64, 512], BF16),
                rrec=sba("rrec", [65, 512], F32),
                sc1=sba("sc1", [128, 8], F32),
                cbias=sba("cbias", [128, 1], F32),
            )
            st = dict(ln_gate={}, kv_ss=0, pf_evac=0, xnb_gate=[0, 0],
                      xnT_gate=[0, 0], qtm_gate=[0, 0], q_evac_ss=[0, 0],
                      cb_ss=0, oa_gate_tt=0, attn_done_tt=0, wo_done_tt=0,
                      grp_free_vv=[0, 0], last_exp_slot=[0, 0, 0],
                      last_pv_slot=[0, 0, 0], sbcast=0, sb2=0, q_done_vv=0)
            run_phase(lambda E: phA_kv_all(E, sh, A, st, init_lx), state=st)
            nq = 4 if A_STEPS >= 6 else 1
            for qq in range(nq):
                if A_STEPS < 2:
                    break
                run_phase(lambda E, qq=qq: phA_q(E, sh, A, st, qq), state=st)
                if qq == 0 and A_STEPS >= 3:
                    run_phase(lambda E: phA_c(E, sh, A, st), state=st)
                if A_STEPS >= 4:
                    run_phase(lambda E, qq=qq: phA_attn(E, sh, A, st, qq), state=st)
                if A_STEPS >= 5:
                    run_phase(lambda E, qq=qq: phA_wo(E, sh, A, st, qq), state=st)
        GATE_KEYS = ["TT", "VV", "SS"] + LANES["LX"] + LANES["LW"] + LANES["STO"]
        gateA = {k2: C[k2].n for k2 in GATE_KEYS}

        if PHASES >= 2:
            with ExitStack() as SB:
                sbb = lambda n_, s_, d_: SB.enter_context(nc.sbuf_tensor(n_, s_, d_))
                Bd = dict(
                    gwT_sb=sbb("gwT_sb", [128, 8 * NE], BF16),
                    laT_sb=sbb("laT_sb", [128, 8 * 32], BF16),
                    lbT_sb=sbb("lbT_sb", [32, FF], BF16),
                    wib_sb=sbb("wib_sb", [128, 32], F32),
                    wob_sb=sbb("wob_sb", [1, D], BF16),
                    gb_sb=sbb("gb_sb", [1, NE], F32),
                    wibuf=sbb("wibuf", [128, 2 * 4096], BF16),
                    wobuf=sbb("wobuf", [128, 2 * 4096], BF16),
                    x1T=sbb("x1T", [128, 8 * 512], BF16),
                    interT=sbb("interT", [128, 32 * 512], BF16),
                    rselT=sbb("rselT", [32, 512], BF16),
                    lgt=sbb("lgtB", [128, NE], F32),
                    sA=sbb("sAB", [128, NE], F32),
                    sB=sbb("sBB", [128, NE], F32),
                    cA=sbb("cAB", [128, NE], F32),
                    cB=sbb("cBB", [128, NE], F32),
                    msk=sbb("mskB", [128, NE], F32),
                    rsel=sbb("rselB", [128, 32], F32),
                    ytmp=sbb("ytmpB", [128, 2 * D], F32),
                )
                run_phase(lambda E: phaseB(E, sh, Bd, gateA))
            gateB = {k2: C[k2].n for k2 in GATE_KEYS}
        else:
            gateB = gateA

        if PHASES >= 3:
            with ExitStack() as SC:
                sbc = lambda n_, s_, d_: SC.enter_context(nc.sbuf_tensor(n_, s_, d_))
                Cd = dict(
                    gbc=sbc("gbcC", [128, D], F32),
                    bbc=sbc("bbcC", [128, D], F32),
                    w_sb=[sbc(f"w1_{i}", [128, 8 * D], BF16) for i in range(2)],
                    xT15=sbc("xT15", [128, D], BF16),
                    tm=[sbc(f"tmC_{i}", [128, D], F32) for i in range(4)],
                    bm=[sbc(f"bmC_{i}", [128, D], BF16) for i in range(4)],
                    fm=[sbc(f"fmC_{i}", [64, 128], BF16) for i in range(4)],
                    fmf=[sbc(f"fmfC_{i}", [64, 128], F32) for i in range(2)],
                    Mh=sbc("Mh", [64, 1024], BF16),
                    Nh=sbc("Nh", [64, 1024], BF16),
                    Ntm=sbc("Ntm", [128, 512], BF16),
                    Af=sbc("Af", [128, 1024], F32),
                    Pf=sbc("Pf", [128, 1024], F32),
                    Pb=sbc("Pb", [128, 1024], BF16),
                    pT=sbc("pTc", [128, 8 * 128], BF16),
                    Ztm=sbc("Ztm", [128, D], BF16),
                    ZT=sbc("ZT", [128, 8 * 128], BF16),
                    zmask=sbc("zmask", [128, D], F32),
                    tin=sbc("tinC", [128, D], F32),
                    att=sbc("attC", [128, D], F32),
                    ssq=sbc("ssqC", [128, 4 * NH], F32),
                    smx=sbc("smxC", [128, 4], F32),
                    wcons={},
                )
                run_phase(lambda E: phaseC(E, sh, Cd, gateB))
            gateC = {k2: C[k2].n for k2 in GATE_KEYS}
        else:
            gateC = gateB

        if PHASES >= 4:
            with ExitStack() as SD:
                sbd = lambda n_, s_, d_: SD.enter_context(nc.sbuf_tensor(n_, s_, d_))
                Dd = dict(
                    x3T=sbd("x3T", [128, 8 * HALF], BF16),
                    rwT_sb=sbd("rwT_sb", [128, 8 * NE], BF16),
                    rb_sb=sbd("rb_sb", [1, NE], F32),
                    gub_sb=sbd("gub_sb", [128, 32], F32),
                    g7mb=sbd("g7mb", [128, 16], F32),
                    gup1=sbd("gup1", [128, 16], F32),
                    dnb_st=sbd("dnb_st", [1, D], F32),
                    dnbn=sbd("dnbn", [1, D], BF16),
                    ncomb=sbd("ncomb", [128, NT * NE], F32),
                    lgt=sbd("lgtD", [128, NE], F32),
                    sA=sbd("sAD", [128, NE], F32),
                    sB=sbd("sBD", [128, NE], F32),
                    cA=sbd("cAD", [128, NE], F32),
                    cB=sbd("cBD", [128, NE], F32),
                    sel=sbd("selD", [128, NE], F32),
                    pex=sbd("pexD", [128, NE], F32),
                    nmx=sbd("nmxD", [128, 2], F32),
                    wgu=[sbd(f"wgu_{i}", [128, 2048], BF16) for i in range(2)],
                    actT=sbd("actT", [128, 16 * 1024], BF16),
                    dnT_sb=sbd("dnTsb", [128, 16 * 1024], BF16),
                    r7=sbd("r7D", [128, 2 * 512], F32),
                    sg=sbd("sgD", [128, 2 * 512], F32),
                    t1=sbd("t1D", [128, 2 * 512], F32),
                )
                run_phase(lambda E: phaseD(E, sh, Dd, gateC))
            gateD = {k2: C[k2].n for k2 in GATE_KEYS}
        else:
            gateD = gateC

        with ExitStack() as SE:
            outb = SE.enter_context(nc.sbuf_tensor("outb", [128, 2 * D], F32))
            run_phase(lambda E: phaseOut(E, sh, outb, gateD))

        def fin(E):
            for name in LANES["STO"]:
                E.w("y", name, C[name].n)
        run_phase(fin)

    return nc


def phA_kv_all(E, sh, A, st, init_lx):
    io, C = sh["io"], sh["C"]
    resid, ones, idb = sh["resid"], sh["ones"], sh["idb"]
    pA, pB, pF = sh["pA"], sh["pB"], sh["pF"]
    gbc, bbc = A["gbc"], A["bbc"]

    if KV_STEPS >= 1:
        wz = E.v(lambda e: e.memset(A["wbuf"][64:128, :], 0.0), "VV")
        E.w("g", "VV", wz)
        st["lw_wq"] = E.dma("LW", "g", lambda e: e.dma_start(
            out=A["wbuf"][:, 0:8 * D], in_=io["wq"][:, :]))
        st["lw_wkv"] = E.dma("LW", "g", lambda e: e.dma_start(
            out=A["wkv_sb"][:], in_=io["wkv"][:, :]))

        E.wt("t", init_lx)
        s1 = bcast_vec(E, sh, io["lnv"][0:1, :], gbc, pB[0])
        st["sbcast"] = bcast_vec(E, sh, io["lnv"][1:2, :], bbc, pB[1])
        vz = E.v(lambda e: e.memset(A["vones"][:], 0.0), "VV")
        E.w("v", "VV", vz)
        E.v(lambda e: e.memset(
            A["vones"][:, :].rearrange("p (k c) -> p k c", c=128)[:, :, 64:65], 1.0))
        E.v(lambda e: e.memset(A["kT"][64:128, :], 0.0))
        E.v(lambda e: e.memset(A["oaT"][64:128, :], 0.0))
        vset = E.v(lambda e: e.memset(A["qhT"][64:128, :], 0.0), "VV")
        st["pad_zero_vv"] = vset

    ownld = []
    for g4 in range(4):
        v = E.dma("LX", "y", lambda e, g4=g4: e.dma_start(
            out=re_dst(resid[:, g4 * 4 * D:(g4 + 1) * 4 * D], D),
            in_=re_src(io["xb"][g4 * 512:(g4 + 1) * 512, :])))
        ownld.append(v)

    stage_vv = [0] * 16
    othld = [0] * 16

    for kt in range(NKT):
        own = kt < NT
        sl = kt % 2
        if own:
            E.wt("v", ownld[kt // 4])
            src = resid[:, kt * D:(kt + 1) * D]
        else:
            if KV_STEPS < 1:
                continue
            gi = kt - NT
            if gi >= 2:
                E.w("y", "VV", stage_vv[gi - 2])
            othld[gi] = E.dma("LX", "y", lambda e, gi=gi: e.dma_start(
                out=A["xst"][:, (gi % 2) * D:((gi % 2) + 1) * D],
                in_=io["xb"][NT * 128 + gi * 128: NT * 128 + (gi + 1) * 128, :]))
            E.wt("v", othld[gi])
            src = A["xst"][:, (gi % 2) * D:((gi % 2) + 1) * D]
        if kt == 0:
            E.w("v", "SS", st["sbcast"])

        if KV_STEPS < 2:
            if not own:
                stage_vv[kt - NT] = 1
            continue
        xnb_ap = A["xnb"][:, sl * D:(sl + 1) * D]
        vln = ln_tile(E, sh, src, xnb_ap, sl, gain=gbc[:, 0:D], bias=bbc[:, 0:D],
                      ln_gate=st["ln_gate"], pre_wait=("TT", st["xnb_gate"][sl]))
        if not own:
            stage_vv[kt - NT] = vln
        if KV_STEPS < 3:
            continue

        E.w("t", "VV", vln)
        E.w("t", "SS", st["pf_evac"])
        ttr = 0
        for c8 in range(8):
            r = E.t(lambda e, c8=c8, xnb_ap=xnb_ap: e.transpose(
                pF[:, c8 * 128:(c8 + 1) * 128], xnb_ap[:, c8 * 128:(c8 + 1) * 128], idb[:]),
                "TT" if c8 == 7 else None)
            ttr = r or ttr
        st["xnb_gate"][sl] = ttr
        xnT_ap = A["xnT"][:, sl * D:(sl + 1) * D]
        E.w("s", "TT", ttr)
        E.w("s", "TT", st["xnT_gate"][sl])
        sevac = E.s(lambda e, xnT_ap=xnT_ap: e.copy(out=xnT_ap, in_=pF[:, 0:D]), "SS")
        st["pf_evac"] = sevac
        if KV_STEPS < 4:
            continue

        E.w("t", "SS", sevac)
        if kt == 0:
            E.wt("t", st["lw_wkv"])
            E.w("t", "VV", vset)
        else:
            E.w("t", "SS", st["kv_ss"])
        tkt = 0
        for c8 in range(8):
            r = E.t(lambda e, c8=c8, xnT_ap=xnT_ap: e.matmul(
                pA[1][0:64, 0:128], A["wkv_sb"][:, c8 * 128:c8 * 128 + 64],
                xnT_ap[:, c8 * 128:(c8 + 1) * 128],
                start=(c8 == 0), stop=(c8 == 7), skip_group_check=True),
                "TT" if c8 == 7 else None)
            tkt = r or tkt
        tvt = 0
        for c8 in range(8):
            r = E.t(lambda e, c8=c8, xnT_ap=xnT_ap: e.matmul(
                pA[2][:, 0:64], xnT_ap[:, c8 * 128:(c8 + 1) * 128],
                A["wkv_sb"][:, c8 * 128 + 64:c8 * 128 + 128],
                start=(c8 == 0), stop=(c8 == 7), skip_group_check=True),
                "TT" if c8 == 7 else None)
            tvt = r or tvt
        st["xnT_gate"][sl] = tvt

        E.w("s", "TT", tkt)
        E.s(lambda e, kt=kt: e.copy(out=A["kT"][0:64, kt * 128:(kt + 1) * 128],
                                    in_=pA[1][0:64, 0:128]))
        E.w("s", "TT", tvt)
        st["kv_ss"] = E.s(lambda e, kt=kt: e.copy(
            out=A["vones"][:, kt * 128:kt * 128 + 64], in_=pA[2][:, 0:64]), "SS")
    st["kv_last_ss"] = st["kv_ss"]


def phA_q(E, sh, A, st, qq):
    io, C = sh["io"], sh["C"]
    resid, idb = sh["resid"], sh["idb"]
    pA, pB, pF = sh["pA"], sh["pB"], sh["pF"]
    gbc, bbc = A["gbc"], A["bbc"]

    if qq > 0:
        E.w("g", "TT", st["wo_done_tt"])
        st["lw_wq"] = E.dma("LW", "g", lambda e: e.dma_start(
            out=A["wbuf"][:, 0:8 * D], in_=io["wq"][:, :]))

    for i4 in range(NQ):
        kt = qq * NQ + i4
        sl = i4 % 2
        src = resid[:, kt * D:(kt + 1) * D]
        xnb_ap = A["xnb"][:, sl * D:(sl + 1) * D]
        vln = ln_tile(E, sh, src, xnb_ap, sl, gain=gbc[:, 0:D], bias=bbc[:, 0:D],
                      ln_gate=st["ln_gate"], pre_wait=("TT", st["xnb_gate"][sl]))
        E.w("t", "VV", vln)
        E.w("t", "SS", st["pf_evac"])
        ttr = 0
        for c8 in range(8):
            r = E.t(lambda e, c8=c8, xnb_ap=xnb_ap: e.transpose(
                pF[:, c8 * 128:(c8 + 1) * 128], xnb_ap[:, c8 * 128:(c8 + 1) * 128], idb[:]),
                "TT" if c8 == 7 else None)
            ttr = r or ttr
        st["xnb_gate"][sl] = ttr
        xnT_ap = A["xnT"][:, sl * D:(sl + 1) * D]
        E.w("s", "TT", ttr)
        E.w("s", "TT", st["xnT_gate"][sl])
        if qq > 0 and i4 == 0:
            E.w("s", "TT", st["attn_done_tt"])
        sevac = E.s(lambda e, xnT_ap=xnT_ap: e.copy(out=xnT_ap, in_=pF[:, 0:D]), "SS")
        st["pf_evac"] = sevac

        E.w("t", "SS", sevac)
        if i4 == 0:
            E.wt("t", st["lw_wq"])
            E.w("t", "SS", st["sbcast"])
            E.w("t", "VV", st["grp_free_vv"][0])
            E.w("t", "VV", st["grp_free_vv"][1])
        E.w("t", "SS", st["q_evac_ss"][i4 % 2])
        tq = 0
        for half in range(2):
            for c8 in range(8):
                r = E.t(lambda e, c8=c8, half=half, i4=i4, xnT_ap=xnT_ap: e.matmul(
                    pB[i4 % 2][:, half * 512:(half + 1) * 512],
                    xnT_ap[:, c8 * 128:(c8 + 1) * 128],
                    A["wbuf"][:, c8 * D + half * 512:c8 * D + (half + 1) * 512],
                    start=(c8 == 0), stop=(c8 == 7), skip_group_check=True),
                    "TT" if (half == 1 and c8 == 7) else None)
                tq = r or tq
        st["xnT_gate"][sl] = tq
        E.w("s", "TT", tq)
        E.w("s", "TT", st["qtm_gate"][sl])
        qtm_ap = A["qtm"][:, sl * D:(sl + 1) * D]
        st["q_evac_ss"][i4 % 2] = E.s(lambda e, i4=i4, qtm_ap=qtm_ap: e.mul(
            out=qtm_ap, in_=pB[i4 % 2][:, 0:D], mul=0.125), "SS")
        sqv = st["q_evac_ss"][i4 % 2]

        if qq == 0:
            E.w("v", "SS", sqv)
            vsq = E.v(lambda e, qtm_ap=qtm_ap: e.tensor_mul(A["sqs"][:, 0:D], qtm_ap, qtm_ap), "VV")
            E.w("v", "VV", vsq)
            E.v(lambda e, i4=i4: e.tensor_reduce(
                A["q2a"][:, i4 * NH:(i4 + 1) * NH],
                A["sqs"][:, 0:D].rearrange("p (h dd) -> p h dd", h=NH),
                axis=AX, op=OP.add), "VV")
            st["q_done_vv"] = C["VV"].n

        for rnd in range(2):
            E.w("t", "SS", st["pf_evac"])
            E.w("t", "SS", sqv)
            trh = 0
            for hh in range(8):
                h = rnd * 8 + hh
                r = E.t(lambda e, h=h, hh=hh, qtm_ap=qtm_ap: e.transpose(
                    pF[0:64, hh * 128:hh * 128 + 128], qtm_ap[:, h * DH:(h + 1) * DH], idb[:]),
                    "TT" if hh == 7 else None)
                trh = r or trh
            E.w("s", "TT", trh)
            pe = 0
            for hh in range(8):
                h = rnd * 8 + hh
                r = E.s(lambda e, h=h, hh=hh, i4=i4: e.copy(
                    out=A["qhT"][0:64, h * 512 + i4 * 128:h * 512 + (i4 + 1) * 128],
                    in_=pF[0:64, hh * 128:hh * 128 + 128]),
                    "SS" if hh == 7 else None)
                pe = r or pe
            st["pf_evac"] = pe
        st["qtm_gate"][sl] = C["TT"].n


def phA_c(E, sh, A, st):
    C = sh["C"]
    pA, ones, idf = sh["pA"], sh["ones"], sh["idf"]

    E.w("v", "SS", st["kv_last_ss"])
    E.w("v", "VV", st["q_done_vv"])
    lastmx = 0
    for piece in range(4):
        if piece > 0:
            E.w("v", "VV", lastmx)
        vv = E.v(lambda e, piece=piece: e.tensor_mul(
            A["sqs"][0:64, 0:D], A["kT"][0:64, piece * D:(piece + 1) * D],
            A["kT"][0:64, piece * D:(piece + 1) * D]), "VV")
        E.w("t", "VV", vv)
        for j in range(2):
            tv = E.t(lambda e, j=j: e.matmul(
                pA[0][0:1, 0:512], ones[0:64, 0:1],
                A["sqs"][0:64, j * 512:(j + 1) * 512],
                start=True, stop=True, skip_group_check=True), "TT")
            E.w("v", "TT", tv)
            vmx = E.v(lambda e: e.reduce_max(A["sc1"][0:1, 4:5], pA[0][0:1, 0:512],
                                             axis=AX), "VV")
            E.w("v", "VV", vmx)
            if piece == 0 and j == 0:
                lastmx = E.v(lambda e: e.tensor_copy(out=A["sc1"][0:1, 0:1],
                                                     in_=A["sc1"][0:1, 4:5]), "VV")
            else:
                lastmx = E.v(lambda e: e.tensor_max(A["sc1"][0:1, 0:1], A["sc1"][0:1, 0:1],
                                                    A["sc1"][0:1, 4:5]), "VV")
            E.w("t", "VV", lastmx)

    E.w("v", "VV", st["q_done_vv"])
    v2 = E.v(lambda e: e.reduce_max(A["sc1"][:, 1:2], A["q2a"][:, 0:NQ * NH], axis=AX), "VV")
    E.w("t", "VV", v2)
    E.w("t", "SS", st["kv_last_ss"])
    t2 = E.t(lambda e: e.transpose(pA[1][0:1, 0:128], A["sc1"][:, 1:2], idf[:]), "TT")
    E.w("v", "TT", t2)
    v3 = E.v(lambda e: e.reduce_max(A["sc1"][0:1, 2:3], pA[1][0:1, 0:128], axis=AX), "VV")
    E.w("v", "VV", max(v3, lastmx))
    v4 = E.v(lambda e: e.tensor_mul(A["sc1"][0:1, 3:4], A["sc1"][0:1, 0:1],
                                    A["sc1"][0:1, 2:3]), "VV")
    E.w("s", "VV", v4)
    sc0 = E.s(lambda e: e.activation(out=A["sc1"][0:1, 4:5], in_=A["sc1"][0:1, 3:4], func=AF.Ln), "SS")
    E.w("s", "SS", sc0)
    sc = E.s(lambda e: e.activation(out=A["sc1"][0:1, 5:6], in_=A["sc1"][0:1, 4:5],
                                    func=AF.Exp, scale=0.5), "SS")
    E.w("v", "SS", sc)
    v5 = E.v(lambda e: e.tensor_scalar_mul(A["sc1"][0:1, 6:7], A["sc1"][0:1, 5:6], -1.0), "VV")
    E.w("t", "VV", v5)
    t3 = E.t(lambda e: e.matmul(pA[1][:, 0:1], ones[0:1, 0:128], A["sc1"][0:1, 6:7],
                                start=True, stop=True, skip_group_check=True), "TT")
    E.w("s", "TT", t3)
    st["cb_ss"] = E.s(lambda e: e.copy(out=A["cbias"][:], in_=pA[1][:, 0:1]), "SS")


def phA_attn(E, sh, A, st, qq):
    C = sh["C"]
    pA, pB = sh["pA"], sh["pB"]
    ones = sh["ones"]

    les, lps = st["last_exp_slot"], st["last_pv_slot"]
    gfv = st["grp_free_vv"]
    NP = KC // 2

    def emit_qk(h, kp, slot):
        qoff = h * 512
        E.w("t", "SS", les[slot])
        for sub in range(2):
            kc = 2 * kp + sub
            r = E.t(lambda e, kc=kc, sub=sub, slot=slot, qoff=qoff: e.matmul(
                pB[slot][:, sub * 512:(sub + 1) * 512],
                A["kT"][:, kc * 128:(kc + 1) * 128],
                A["qhT"][:, qoff:qoff + 512],
                start=True, stop=True, skip_group_check=True),
                "TT" if sub == 1 else None)
        return r

    def emit_exp(qkv, slot, pslot):
        E.w("s", "TT", qkv)
        E.w("s", "TT", lps[pslot])
        les[slot] = E.s(lambda e, slot=slot, pslot=pslot: e.activation(
            out=A["pTs"][:, pslot * 1024:(pslot + 1) * 1024],
            in_=pB[slot][:, 0:1024], func=AF.Exp,
            bias=A["cbias"][:, 0:1], scale=1.0), "SS")
        return les[slot]

    def emit_pv(h, kp, pslot, ev):
        pva = pA[h % 2]
        E.w("t", "SS", ev)
        for sub in range(2):
            kc = 2 * kp + sub
            r = E.t(lambda e, kc=kc, sub=sub, pslot=pslot, pva=pva: e.matmul(
                pva[:, 0:512], A["vones"][:, kc * 128:kc * 128 + 128],
                A["pTs"][:, pslot * 1024 + sub * 512:pslot * 1024 + (sub + 1) * 512],
                start=(kc == 0), stop=(kc == KC - 1), skip_group_check=True),
                "TT" if sub == 1 else None)
        lps[pslot] = r
        return r

    def epilogue(h, pvd):
        pva = pA[h % 2]
        qoff = h * 512
        E.w("v", "TT", pvd)
        E.w("v", "VV", st.get("rrec_free", 0))
        vr = E.v(lambda e, pva=pva: e.reciprocal(A["rrec"][64:65, 0:512],
                                                 pva[64:65, 0:512]), "VV")
        E.w("t", "VV", vr)
        E.w("t", "VV", st.get("bc_free_vv", 0))
        tb = E.t(lambda e: e.matmul(
            pA[2][0:64, 0:512], ones[64:65, 0:64], A["rrec"][64:65, 0:512],
            start=True, stop=True, skip_group_check=True,
            tile_position=(64, 0)), "TT")
        E.w("s", "TT", pvd)
        E.w("s", "VV", gfv[1 - h % 2])
        su = E.s(lambda e, pva=pva: e.copy(out=A["und"][0:64, 0:512],
                                           in_=pva[0:64, 0:512]), "SS")
        E.w("v", "TT", tb)
        E.w("v", "SS", su)
        gfv[h % 2] = E.v(lambda e, qoff=qoff: e.tensor_mul(
            A["oaT"][0:64, qoff:qoff + 512], A["und"][0:64, 0:512],
            pA[2][0:64, 0:512]), "VV")
        st["bc_free_vv"] = gfv[h % 2]
        st["rrec_free"] = vr

    pending = None
    for h in range(NH):
        pva = pA[h % 2]
        E.w("t", "VV", gfv[h % 2])
        if h < 2:
            E.w("t", "SS", st["q_evac_ss"][0])
            E.w("t", "SS", st["q_evac_ss"][1])
        if qq == 0 and h == 0:
            E.w("t", "SS", st["cb_ss"])
            E.w("t", "VV", st["pad_zero_vv"])
        if qq > 0 and h == 0:
            E.w("t", "TT", st["oa_gate_tt"])
        qks = {}
        evs = {}
        qks[0] = emit_qk(h, 0, 0)
        evs[0] = emit_exp(qks[0], 0, 0)
        for kp in range(NP):
            if kp + 1 < NP:
                qks[kp + 1] = emit_qk(h, kp + 1, (kp + 1) % 2)
                evs[kp + 1] = emit_exp(qks[kp + 1], (kp + 1) % 2, (kp + 1) % 3)
            emit_pv(h, kp, kp % 3, evs[kp])
            if kp == 1 and pending is not None:
                epilogue(*pending)
                pending = None
        pending = (h, lps[(NP - 1) % 3])
    epilogue(*pending)
    st["oa_done_vv"] = C["VV"].n
    st["attn_done_tt"] = C["TT"].n


def phA_wo(E, sh, A, st, qq):
    io, C = sh["io"], sh["C"]
    pB, resid = sh["pB"], sh["resid"]

    E.w("g", "TT", st["qtm_gate"][0])
    E.w("g", "TT", st["qtm_gate"][1])
    lw_wo = E.dma("LW", "g", lambda e: e.dma_start(
        out=A["wbuf"][0:64, 0:NH * D], in_=io["wo"][:, :]))
    if qq == 0:
        E.w("t", "VV", st["grp_free_vv"][0])
        E.w("t", "VV", st["grp_free_vv"][1])
        s1 = bcast_vec(E, sh, io["lnv"][2:3, :], A["gbc2"], pB[0])
        st["sb2"] = bcast_vec(E, sh, io["lnv"][3:4, :], A["bbc2"], pB[1])
    E.wt("t", lw_wo)
    E.w("t", "VV", st["oa_done_vv"])
    E.w("t", "SS", st["sb2"])

    evac_vv = [st["grp_free_vv"][0], st["grp_free_vv"][1]]
    for i4 in range(NQ):
        t = qq * NQ + i4
        pw = pB[i4 % 2]
        E.w("t", "VV", evac_vv[i4 % 2])
        tw = 0
        for h in range(NH):
            for half in range(2):
                r = E.t(lambda e, h=h, half=half, i4=i4, pw=pw: e.matmul(
                    pw[:, half * 512:(half + 1) * 512],
                    A["oaT"][:, h * 512 + i4 * 128:h * 512 + (i4 + 1) * 128],
                    A["wbuf"][:, h * D + half * 512:h * D + (half + 1) * 512],
                    start=(h == 0), stop=(h == NH - 1), skip_group_check=True),
                    "TT" if (h == NH - 1 and half == 1) else None)
                tw = r or tw
        E.w("v", "TT", tw)
        if i4 == 0:
            E.w("v", "SS", st["sb2"])
        vln = ln_tile(E, sh, pw[:, 0:D], A["sqs"][:, 0:D], i4 % 2,
                      gain=A["gbc2"][:, 0:D], bias=A["bbc2"][:, 0:D],
                      ln_gate=st["ln_gate"])
        E.w("v", "VV", vln)
        vy = E.v(lambda e, t=t: e.tensor_add(A["sqs"][:, 0:D], A["sqs"][:, 0:D],
                                             resid[:, t * D:(t + 1) * D]), "VV")
        evac_vv[i4 % 2] = vy
        E.w("v", "VV", vy)
        ln_tile(E, sh, A["sqs"][:, 0:D], resid[:, t * D:(t + 1) * D], i4 % 2,
                ln_gate=st["ln_gate"])
    st["wo_done_tt"] = C["TT"].n
    st["oa_gate_tt"] = C["TT"].n
    st["grp_free_vv"] = [C["VV"].n, C["VV"].n]


def phaseB(E, sh, Bd, gate):
    io, C = sh["io"], sh["C"]
    pA, pB, pF = sh["pA"], sh["pB"], sh["pF"]
    resid, ones, ones_bf = sh["resid"], sh["ones"], sh["ones_bf"]
    idf = sh["idf"]

    phase_gate(E, gate)
    lw_gw = E.dma("LW", "g", lambda e: e.dma_start(
        out=Bd["gwT_sb"][:], in_=io["gwT"][:, :]))
    lw_la = E.dma("LW", "g", lambda e: e.dma_start(
        out=Bd["laT_sb"][:], in_=io["laT"][:, :]))
    lw_lb = E.dma("LW", "g", lambda e: e.dma_start(out=Bd["lbT_sb"][:], in_=io["lbT"][:, :]))
    lw_wib = E.dma("LW", "g", lambda e: e.dma_start(out=Bd["wib_sb"][:], in_=io["wib"][:, :]))
    lw_wob = E.dma("LW", "g", lambda e: e.dma_start(out=Bd["wob_sb"][:], in_=io["wob"][:, :]))
    lx_gb = E.dma("LX", "y", lambda e: e.dma_start(out=Bd["gb_sb"][:], in_=io["gbv"][:, :]))

    wo_done_tt = gate["TT"]
    prev_vrs = gate["VV"]
    ln_gate = {}
    wi_lw = {}
    wi_cons = {}
    wo_pb_vv = [gate["VV"], gate["VV"]]
    def emit_wi(E, gid):
        grp = gid % 8
        E.w("g", "TT", wi_cons.get(gid - 2, 0))
        wi_lw[gid] = E.dma("LW", "g", lambda e, grp=grp, gid=gid: e.dma_start(
            out=Bd["wibuf"][:, (gid % 2) * 4096:((gid % 2) + 1) * 4096],
            in_=io["wiT"][grp, :, :]))

    for tg in range(4):
        emit_wi(E, tg * 8 + 0)
        emit_wi(E, tg * 8 + 1)

        rselT_ss = 0
        for tl in range(4):
            t = tg * 4 + tl
            src = resid[:, t * D:(t + 1) * D]
            if t == 0:
                for k2 in ("TT", "VV", "SS"):
                    E.w("t", k2, gate[k2])
            E.w("t", "TT", wo_done_tt)
            E.w("t", "SS", Bd.get("x1T_ss", {}).get(tl, 0) if False else 0)
            ttr = 0
            for c8 in range(8):
                r = E.t(lambda e, c8=c8, src=src: e.transpose(
                    pA[c8 // 4][:, (c8 % 4) * 128:(c8 % 4 + 1) * 128],
                    src[:, c8 * 128:(c8 + 1) * 128], idf[:]),
                    "TT" if c8 == 7 else None)
                ttr = r or ttr
            E.w("s", "TT", ttr)
            sx = 0
            for c8 in range(8):
                r = E.s(lambda e, c8=c8, tl=tl: e.copy(
                    out=Bd["x1T"][:, c8 * 512 + tl * 128:c8 * 512 + (tl + 1) * 128],
                    in_=pA[c8 // 4][:, (c8 % 4) * 128:(c8 % 4 + 1) * 128]),
                    "SS" if c8 == 7 else None)
                sx = r or sx

            E.w("t", "SS", sx)
            if t == 0:
                E.wt("t", lw_gw)
                E.wt("t", lw_la)
                E.wt("t", lx_gb)
            for c8 in range(8):
                E.t(lambda e, c8=c8, tl=tl: e.matmul(
                    pA[2][:, 0:NE],
                    Bd["x1T"][:, c8 * 512 + tl * 128:c8 * 512 + (tl + 1) * 128],
                    Bd["gwT_sb"][:, c8 * NE:(c8 + 1) * NE],
                    start=(c8 == 0), stop=False, skip_group_check=True))
            tlg = E.t(lambda e: e.matmul(
                pA[2][:, 0:NE], ones[0:1, 0:128], Bd["gb_sb"][0:1, 0:NE],
                start=False, stop=True, skip_group_check=True), "TT")
            E.w("t", "VV", prev_vrs)
            tra = 0
            for c8 in range(8):
                r = E.t(lambda e, c8=c8, tl=tl: e.matmul(
                    pA[1][:, 0:32],
                    Bd["x1T"][:, c8 * 512 + tl * 128:c8 * 512 + (tl + 1) * 128],
                    Bd["laT_sb"][:, c8 * 32:(c8 + 1) * 32],
                    start=(c8 == 0), stop=(c8 == 7), skip_group_check=True),
                    "TT" if c8 == 7 else None)
                tra = r or tra

            E.w("v", "TT", tlg)
            rank_counts(E, sh, Bd, pA[2][:, 0:NE])
            vmk = E.v(lambda e: e.tensor_scalar(out=Bd["msk"][:], in0=Bd["cA"][:],
                                                scalar1=0.5, scalar2=None,
                                                op0=OP.is_lt), "VV")
            E.w("v", "VV", vmk)
            E.w("v", "TT", tra)
            vrs = 0
            for ex in range(NE):
                r = E.v(lambda e, ex=ex: e.tensor_scalar(
                    out=Bd["rsel"][:, ex * 4:(ex + 1) * 4],
                    in0=pA[1][:, ex * 4:(ex + 1) * 4],
                    scalar1=Bd["msk"][:, ex:ex + 1], scalar2=None, op0=OP.mult),
                    "VV" if ex == NE - 1 else None)
                vrs = r or vrs
            prev_vrs = vrs
            E.w("t", "VV", vrs)
            trs = E.t(lambda e: e.transpose(pA[2][0:32, 128:256], Bd["rsel"][:, 0:32],
                                            idf[:]), "TT")
            E.w("s", "TT", trs)
            rselT_ss = E.s(lambda e, tl=tl: e.copy(
                out=Bd["rselT"][0:32, tl * 128:(tl + 1) * 128],
                in_=pA[2][0:32, 128:256]), "SS")

        relu_ss = [0, 0]
        for p in range(32):
            gid = tg * 8 + p // 4
            pg = pA[p % 2]
            E.wt("t", wi_lw[gid])
            E.w("t", "SS", relu_ss[p % 2])
            if p == 0:
                E.wt("t", lw_wib)
                E.wt("t", lw_lb)
                E.w("t", "SS", rselT_ss)
            for c8 in range(8):
                E.t(lambda e, c8=c8, p=p, gid=gid, pg=pg: e.matmul(
                    pg[:, 0:512],
                    Bd["wibuf"][:, (gid % 2) * 4096 + c8 * 512 + (p % 4) * 128:
                                (gid % 2) * 4096 + c8 * 512 + (p % 4 + 1) * 128],
                    Bd["x1T"][:, c8 * 512:(c8 + 1) * 512],
                    start=(c8 == 0), stop=False, skip_group_check=True))
            tbp = E.t(lambda e, p=p, pg=pg: e.matmul(
                pg[:, 0:512], Bd["lbT_sb"][0:32, p * 128:(p + 1) * 128],
                Bd["rselT"][0:32, 0:512],
                start=False, stop=True, skip_group_check=True), "TT")
            if p % 4 == 3:
                wi_cons[gid] = tbp
                if p // 4 + 2 <= 7:
                    emit_wi(E, tg * 8 + p // 4 + 2)
            E.w("s", "TT", tbp)
            relu_ss[p % 2] = E.s(lambda e, p=p, pg=pg: e.activation(
                out=Bd["interT"][:, p * 512:(p + 1) * 512], in_=pg[:, 0:512],
                func=AF.Relu, bias=Bd["wib_sb"][:, p:p + 1]), "SS")

        wo_lw = {}
        wo_cons = {}

        def emit_wo(E, wid):
            pg8 = wid % 8
            E.w("g", "TT", wo_cons.get(wid - 2, 0))
            wo_lw[wid] = E.dma("LW", "g", lambda e, pg8=pg8, wid=wid: e.dma_start(
                out=Bd["wobuf"][:, (wid % 2) * 4096:((wid % 2) + 1) * 4096],
                in_=io["woT"][pg8, :, :]))

        for tlp in range(2):
            emit_wo(E, tlp * 8 + 0)
            emit_wo(E, tlp * 8 + 1)
            for pg8 in range(8):
                wid = tlp * 8 + pg8
                E.wt("t", wo_lw[wid])
                if pg8 == 0:
                    E.w("t", "SS", relu_ss[0])
                    E.w("t", "SS", relu_ss[1])
                    E.wt("t", lw_wob)
                    E.w("t", "VV", wo_pb_vv[0])
                    E.w("t", "VV", wo_pb_vv[1])
                wcv = 0
                for pp in range(4):
                    p = pg8 * 4 + pp
                    for tli in range(2):
                        tl = tlp * 2 + tli
                        for half in range(2):
                            r = E.t(lambda e, p=p, pp=pp, tl=tl, tli=tli, half=half, wid=wid: e.matmul(
                                pB[tli][:, half * 512:(half + 1) * 512],
                                Bd["interT"][:, p * 512 + tl * 128:p * 512 + (tl + 1) * 128],
                                Bd["wobuf"][:, (wid % 2) * 4096 + pp * 1024 + half * 512:
                                            (wid % 2) * 4096 + pp * 1024 + (half + 1) * 512],
                                start=(p == 0), stop=False, skip_group_check=True),
                                "TT" if (pp == 3 and tli == 1 and half == 1) else None)
                            wcv = r or wcv
                wo_cons[wid] = wcv
                if pg8 + 2 <= 7:
                    emit_wo(E, tlp * 8 + pg8 + 2)
            twn = 0
            for tli in range(2):
                for half in range(2):
                    r = E.t(lambda e, tli=tli, half=half: e.matmul(
                        pB[tli][:, half * 512:(half + 1) * 512], ones_bf[0:1, 0:128],
                        Bd["wob_sb"][0:1, half * 512:(half + 1) * 512],
                        start=False, stop=(half == 1), skip_group_check=True),
                        "TT" if (tli == 1 and half == 1) else None)
                    twn = r or twn
            for tli in range(2):
                t = tg * 4 + tlp * 2 + tli
                E.w("v", "TT", twn)
                vy = E.v(lambda e, t=t, tli=tli: e.tensor_add(
                    Bd["ytmp"][:, tli * D:(tli + 1) * D], pB[tli][:, 0:D],
                    resid[:, t * D:(t + 1) * D]), "VV")
                E.w("v", "VV", vy)
                vf = ln_tile(E, sh, Bd["ytmp"][:, tli * D:(tli + 1) * D],
                             resid[:, t * D:(t + 1) * D], tli, ln_gate=ln_gate)
                wo_pb_vv[tli] = vy
        wo_done_tt = C["TT"].n


def phaseC(E, sh, Cd, gate):
    io, C = sh["io"], sh["C"]
    pA, pB, pF = sh["pA"], sh["pB"], sh["pF"]
    resid, ones, idf, idb = sh["resid"], sh["ones"], sh["idf"], sh["idb"]
    gbc, bbc, rflag = Cd["gbc"], Cd["bbc"], sh["rflag"]

    phase_gate(E, gate)
    lww = [0] * 5
    for i in range(2):
        lww[i] = E.dma("LW", "g", lambda e, i=i: e.dma_start(
            out=Cd["w_sb"][i % 2][:], in_=io["w1"][i, :, :]))

    for k2 in ("TT", "VV", "SS"):
        E.w("t", k2, gate[k2])
    sb2a = bcast_vec(E, sh, io["lnv"][4:5, :], gbc, pB[0])
    sb2 = bcast_vec(E, sh, io["lnv"][5:6, :], bbc, pB[1])

    src15 = resid[:, 15 * D:16 * D]
    ttr = 0
    for c8 in range(8):
        r = E.t(lambda e, c8=c8: e.transpose(
            pA[c8 // 4][:, (c8 % 4) * 128:(c8 % 4 + 1) * 128],
            src15[:, c8 * 128:(c8 + 1) * 128], idf[:]), "TT" if c8 == 7 else None)
        ttr = r or ttr
    E.w("s", "TT", ttr)
    sx15 = 0
    for c8 in range(8):
        r = E.s(lambda e, c8=c8: e.copy(
            out=Cd["xT15"][:, c8 * 128:(c8 + 1) * 128],
            in_=pA[c8 // 4][:, (c8 % 4) * 128:(c8 % 4 + 1) * 128]),
            "SS" if c8 == 7 else None)
        sx15 = r or sx15

    prj_ss = []
    for i in range(4):
        E.w("t", "SS", sx15 if i == 0 else 0)
        E.wt("t", lww[i])
        pw = pB[i % 2]
        if i == 0:
            E.w("t", "SS", sb2)
        if i >= 2:
            E.w("t", "SS", prj_ss[i - 2])
        tp = 0
        for half in range(2):
            for c8 in range(8):
                r = E.t(lambda e, c8=c8, half=half, i=i, pw=pw: e.matmul(
                    pw[:, half * 512:(half + 1) * 512],
                    Cd["xT15"][:, c8 * 128:(c8 + 1) * 128],
                    Cd["w_sb"][i % 2][:, c8 * D + half * 512:c8 * D + (half + 1) * 512],
                    start=(c8 == 0), stop=(c8 == 7), skip_group_check=True),
                    "TT" if (half == 1 and c8 == 7) else None)
                tp = r or tp
        if i + 2 <= 4:
            E.w("g", "TT", tp)
            lww[i + 2] = E.dma("LW", "g", lambda e, i=i: e.dma_start(
                out=Cd["w_sb"][i % 2][:], in_=io["w1"][i + 2, :, :]))
        E.w("s", "TT", tp)
        prj_ss.append(E.s(lambda e, i=i, pw=pw: e.copy(out=Cd["tm"][i][:, 0:D],
                                                       in_=pw[:, 0:D]), "SS"))

    for i in range(3):
        E.w("v", "SS", prj_ss[i])
        vs = E.v(lambda e, i=i: e.tensor_mul(Cd["Af"][:, 0:D], Cd["tm"][i][:, 0:D],
                                             Cd["tm"][i][:, 0:D]), "VV")
        E.w("v", "VV", vs)
        vr = E.v(lambda e, i=i: e.tensor_reduce(
            Cd["ssq"][:, i * NH:(i + 1) * NH],
            Cd["Af"][:, 0:D].rearrange("p (h dd) -> p h dd", h=NH),
            axis=AX, op=OP.add), "VV")
        E.w("s", "VV", vr)
        s2a = E.s(lambda e, i=i: e.activation(out=Cd["ssq"][:, 3 * NH:4 * NH],
                                              in_=Cd["ssq"][:, i * NH:(i + 1) * NH], func=AF.Ln), "SS")
        E.w("s", "SS", s2a)
        s2 = E.s(lambda e, i=i: e.activation(out=Cd["ssq"][:, i * NH:(i + 1) * NH],
                                             in_=Cd["ssq"][:, 3 * NH:4 * NH],
                                             func=AF.Exp, scale=0.5), "SS")
        E.w("v", "SS", s2)
        va = E.v(lambda e, i=i: e.tensor_scalar_add(
            Cd["ssq"][:, i * NH:(i + 1) * NH], Cd["ssq"][:, i * NH:(i + 1) * NH], 1e-7), "VV")
        E.w("v", "VV", va)
        vb = E.v(lambda e, i=i: e.reciprocal(Cd["ssq"][:, i * NH:(i + 1) * NH],
                                             Cd["ssq"][:, i * NH:(i + 1) * NH]), "VV")
        E.w("v", "VV", vb)
        for h in range(NH):
            E.v(lambda e, i=i, h=h: e.tensor_scalar(
                out=Cd["bm"][i][:, h * DH:(h + 1) * DH],
                in0=Cd["tm"][i][:, h * DH:(h + 1) * DH],
                scalar1=Cd["ssq"][:, i * NH + h:i * NH + h + 1],
                scalar2=None, op0=OP.mult), "VV" if h == NH - 1 else None)
    E.w("v", "SS", prj_ss[3])
    bm_vv = E.v(lambda e: e.tensor_copy(out=Cd["bm"][3][:, 0:D], in_=Cd["tm"][3][:, 0:D]), "VV")

    tz_prev = 0
    vmn_prev = bm_vv
    ga_prev = 0
    vm_prev = 0
    zev_ss = 0
    for h in range(NH):
        E.w("t", "VV", vmn_prev)
        E.w("t", "SS", zev_ss)
        tf = 0
        for i in range(4):
            r = E.t(lambda e, i=i, h=h: e.transpose(
                pF[0:64, i * 128:(i + 1) * 128], Cd["bm"][i][:, h * DH:(h + 1) * DH], idb[:]),
                "TT" if i == 3 else None)
            tf = r or tf
        E.w("s", "TT", tf)
        for i in range(4):
            E.s(lambda e, i=i: e.copy(out=Cd["fm"][i][0:64, 0:128],
                                      in_=pF[0:64, i * 128:(i + 1) * 128]))
        E.s(lambda e: e.copy(out=Cd["fmf"][0][0:64, 0:128], in_=pF[0:64, 1 * 128:2 * 128]))
        sf = E.s(lambda e: e.copy(out=Cd["fmf"][1][0:64, 0:128],
                                  in_=pF[0:64, 3 * 128:4 * 128]), "SS")
        E.w("v", "SS", sf)
        for j in range(32):
            E.v(lambda e, j=j: e.tensor_scalar(
                out=Cd["Mh"][0:64, j * 32:(j + 1) * 32],
                in0=Cd["fm"][2][0:64, 96:128],
                scalar1=Cd["fmf"][0][0:64, 96 + j:97 + j], scalar2=None, op0=OP.mult))
        vmn = 0
        for j in range(32):
            r = E.v(lambda e, j=j: e.tensor_scalar(
                out=Cd["Nh"][0:64, j * 32:(j + 1) * 32],
                in0=Cd["fm"][3][0:64, 96:128],
                scalar1=Cd["fmf"][1][0:64, 96 + j:97 + j], scalar2=None, op0=OP.mult),
                "VV" if j == 31 else None)
            vmn = r or vmn
        vmn_prev = vmn
        E.w("t", "VV", vmn)
        ta = 0
        for half in range(2):
            r = E.t(lambda e, half=half: e.matmul(
                pB[0][:, half * 512:(half + 1) * 512], Cd["fm"][0][0:64, 0:128],
                Cd["Mh"][0:64, half * 512:(half + 1) * 512],
                start=True, stop=True, skip_group_check=True),
                "TT" if half == 1 else None)
            ta = r or ta
        E.w("s", "TT", ta)
        E.w("s", "GG", ga_prev)
        E.w("s", "VV", vm_prev)
        sa = E.s(lambda e: e.copy(out=Cd["Af"][:, 0:1024], in_=pB[0][:, 0:1024]), "SS")
        E.w("g", "SS", sa)
        ga0 = E.g(lambda e: e.affine_select(out=Cd["Af"][:], in_=Cd["Af"][:],
                                            pattern=[[-1, 32], [0, 32]], base=96,
                                            channel_multiplier=1,
                                            compare_op=OP.is_ge, fill=-1e30), "GG")
        E.w("g", "GG", ga0)
        ga = E.g(lambda e: e.affine_select(out=Cd["Af"][:], in_=Cd["Af"][:],
                                           pattern=[[0, 32], [-1, 32]], base=96,
                                           channel_multiplier=1,
                                           compare_op=OP.is_ge, fill=-1e30), "GG")
        ga_prev = ga
        E.w("v", "GG", ga)
        vm = E.v(lambda e: e.reduce_max(Cd["smx"][:, 0:1], Cd["Af"][:], axis=AX,
                                        negate=True), "VV")
        E.w("v", "VV", vm)
        vm8 = E.v(lambda e: e.tensor_scalar_mul(Cd["smx"][:, 1:2], Cd["smx"][:, 0:1],
                                                0.125), "VV")
        vm_prev = vm8
        E.w("s", "VV", vm8)
        sp = E.s(lambda e: e.activation(out=Cd["Pf"][:, 0:1024], in_=Cd["Af"][:],
                                        func=AF.Exp, bias=Cd["smx"][:, 1:2], scale=0.125,
                                        accum_out=Cd["smx"][:, 2:3]), "SS")
        E.w("v", "SS", sp)
        vrp = E.v(lambda e: e.reciprocal(Cd["smx"][:, 3:4], Cd["smx"][:, 2:3]), "VV")
        E.w("v", "VV", vrp)
        vpb = E.v(lambda e: e.tensor_scalar(out=Cd["Pb"][:], in0=Cd["Pf"][:, 0:1024],
                                            scalar1=Cd["smx"][:, 3:4], scalar2=None,
                                            op0=OP.mult), "VV")
        E.w("t", "VV", vpb)
        E.w("t", "TT", tz_prev)
        tpb = 0
        for c8 in range(8):
            r = E.t(lambda e, c8=c8: e.transpose(pF[:, c8 * 128:(c8 + 1) * 128],
                                                 Cd["Pb"][:, c8 * 128:(c8 + 1) * 128], idb[:]),
                    "TT" if c8 == 7 else None)
            tpb = r or tpb
        E.w("s", "TT", tpb)
        spb = E.s(lambda e: e.copy(out=Cd["pT"][:, 0:1024], in_=pF[:, 0:1024]), "SS")
        E.w("t", "SS", spb)
        tnm = 0
        for c8 in range(8):
            r = E.t(lambda e, c8=c8: e.transpose(pF[:, c8 * 64:(c8 + 1) * 64],
                                                 Cd["Nh"][0:64, c8 * 128:(c8 + 1) * 128], idb[0:64, 0:64]),
                    "TT" if c8 == 7 else None)
            tnm = r or tnm
        E.w("s", "TT", tnm)
        snm = E.s(lambda e: e.copy(out=Cd["Ntm"][:, 0:512], in_=pF[:, 0:512]), "SS")
        E.w("t", "SS", snm)
        tz = 0
        for c8 in range(8):
            r = E.t(lambda e, c8=c8: e.matmul(
                pA[0][:, 0:64], Cd["pT"][:, c8 * 128:(c8 + 1) * 128],
                Cd["Ntm"][:, c8 * 64:(c8 + 1) * 64],
                start=(c8 == 0), stop=(c8 == 7), skip_group_check=True),
                "TT" if c8 == 7 else None)
            tz = r or tz
        tz_prev = tz
        E.w("s", "TT", tz)
        zev_ss = E.s(lambda e, h=h: e.copy(out=Cd["Ztm"][:, h * DH:(h + 1) * DH],
                                           in_=pA[0][:, 0:64]), "SS")

    E.w("t", "SS", zev_ss)
    E.wt("t", lww[4])
    tzt = 0
    for c8 in range(8):
        r = E.t(lambda e, c8=c8: e.transpose(pF[:, c8 * 128:(c8 + 1) * 128],
                                             Cd["Ztm"][:, c8 * 128:(c8 + 1) * 128], idb[:]),
                "TT" if c8 == 7 else None)
        tzt = r or tzt
    E.w("s", "TT", tzt)
    szt = E.s(lambda e: e.copy(out=Cd["ZT"][:, 0:1024], in_=pF[:, 0:1024]), "SS")
    E.w("t", "SS", szt)
    tzp = 0
    for half in range(2):
        for c8 in range(8):
            r = E.t(lambda e, c8=c8, half=half: e.matmul(
                pB[0][:, half * 512:(half + 1) * 512],
                Cd["ZT"][:, c8 * 128:(c8 + 1) * 128],
                Cd["w_sb"][0][:, c8 * D + half * 512:c8 * D + (half + 1) * 512],
                start=(c8 == 0), stop=(c8 == 7), skip_group_check=True),
                "TT" if (half == 1 and c8 == 7) else None)
            tzp = r or tzp
    E.w("s", "TT", tzp)
    szp = E.s(lambda e: e.mul(out=Cd["zmask"][:, 0:D], in_=pB[0][:, 0:D],
                              mul=rflag[:, 0:1]), "SS")
    E.w("v", "SS", szp)
    vti = E.v(lambda e: e.tensor_add(Cd["tin"][:, 0:D], Cd["zmask"][:, 0:D],
                                     resid[:, 15 * D:16 * D]), "VV")

    ln_gate = {}
    for t in range(NT):
        src = Cd["tin"][:, 0:D] if t == 15 else resid[:, t * D:(t + 1) * D]
        pre = vti if t == 15 else 0
        if t == 0:
            E.w("v", "SS", sb2)
        va = ln_tile(E, sh, src, Cd["att"][:, 0:D], t % 2,
                     gain=gbc[:, 0:D], bias=bbc[:, 0:D], ln_gate=ln_gate,
                     pre_v_wait=pre)
        E.w("v", "VV", va)
        vy = E.v(lambda e, t=t: e.tensor_add(Cd["Pf"][:, 0:D], Cd["att"][:, 0:D],
                                             resid[:, t * D:(t + 1) * D]), "VV")
        E.w("v", "VV", vy)
        ln_tile(E, sh, Cd["Pf"][:, 0:D], resid[:, t * D:(t + 1) * D], t % 2,
                ln_gate=ln_gate)


def phaseD(E, sh, Dd, gate):
    io, C = sh["io"], sh["C"]
    pA, pB, pF = sh["pA"], sh["pB"], sh["pF"]
    resid, ones, ones_bf, idf = sh["resid"], sh["ones"], sh["ones_bf"], sh["idf"]

    phase_gate(E, gate)
    lw_rw = E.dma("LW", "g", lambda e: e.dma_start(
        out=Dd["rwT_sb"][:], in_=io["rwT"][:, :]))
    lx_rb = E.dma("LX", "y", lambda e: e.dma_start(out=Dd["rb_sb"][:], in_=io["rbv"][:, :]))

    rt_ss = 0
    sx_prev = 0
    for t in range(NT):
        if t == 0:
            for k2 in ("TT", "VV", "SS"):
                E.w("t", k2, gate[k2])
        E.w("t", "SS", sx_prev)
        ttr = 0
        for c8 in range(8):
            r = E.t(lambda e, c8=c8, t=t: e.transpose(
                pA[c8 // 4][:, (c8 % 4) * 128:(c8 % 4 + 1) * 128],
                resid[:, t * D + c8 * 128:t * D + (c8 + 1) * 128], idf[:]),
                "TT" if c8 == 7 else None)
            ttr = r or ttr
        E.w("s", "TT", ttr)
        sx = 0
        for c8 in range(8):
            r = E.s(lambda e, c8=c8, t=t: e.copy(
                out=Dd["x3T"][:, c8 * HALF + t * 128:c8 * HALF + (t + 1) * 128],
                in_=pA[c8 // 4][:, (c8 % 4) * 128:(c8 % 4 + 1) * 128]),
                "SS" if c8 == 7 else None)
            sx = r or sx
        sx_prev = sx
        E.w("t", "SS", sx)
        E.w("t", "SS", rt_ss)
        if t == 0:
            E.wt("t", lw_rw)
            E.wt("t", lx_rb)
        for c8 in range(8):
            E.t(lambda e, c8=c8, t=t: e.matmul(
                pA[2][:, 0:NE],
                Dd["x3T"][:, c8 * HALF + t * 128:c8 * HALF + (t + 1) * 128],
                Dd["rwT_sb"][:, c8 * NE:(c8 + 1) * NE],
                start=(c8 == 0), stop=False, skip_group_check=True))
        tlg = E.t(lambda e: e.matmul(pA[2][:, 0:NE], ones[0:1, 0:128],
                                     Dd["rb_sb"][0:1, 0:NE],
                                     start=False, stop=True, skip_group_check=True), "TT")
        E.w("v", "TT", tlg)
        rank_counts(E, sh, Dd, pA[2][:, 0:NE])
        E.v(lambda e: e.tensor_scalar(out=Dd["sel"][:], in0=Dd["cA"][:],
                                      scalar1=3.5, scalar2=None, op0=OP.is_lt))
        vnm = E.v(lambda e: e.reduce_max(Dd["nmx"][:, 0:1], Dd["lgt"][:], axis=AX,
                                         negate=True), "VV")
        E.w("s", "VV", vnm)
        rt_ss = E.s(lambda e: e.activation(out=Dd["pex"][:], in_=Dd["lgt"][:],
                                           func=AF.Exp, bias=Dd["nmx"][:, 0:1],
                                           scale=1.0), "SS")
        E.w("v", "SS", rt_ss)
        vpm = E.v(lambda e: e.tensor_mul(Dd["pex"][:], Dd["pex"][:], Dd["sel"][:]), "VV")
        E.w("v", "VV", vpm)
        vdn = E.v(lambda e: e.reduce_sum(Dd["nmx"][:, 1:2], Dd["pex"][:], axis=AX), "VV")
        E.w("v", "VV", vdn)
        vrc = E.v(lambda e: e.reciprocal(Dd["nmx"][:, 1:2], Dd["nmx"][:, 1:2]), "VV")
        E.w("v", "VV", vrc)
        E.v(lambda e, t=t: e.tensor_scalar(out=Dd["ncomb"][:, t * NE:(t + 1) * NE],
                                           in0=Dd["pex"][:],
                                           scalar1=Dd["nmx"][:, 1:2], scalar2=-1.0,
                                           op0=OP.mult, op1=OP.mult), "VV")
    comb_vv = C["VV"].n

    st_vv = comb_vv
    pair_ss = [0, 0]
    pair_vv = [0, 0]
    act_vv = [0, 0]
    dn_vv = [0, 0]
    dn_cons_tt = 0
    gub_cons_tt = 0
    dnb_cons = 0
    gub_free_ss = 0
    gub_free_vv = 0
    gup1_free = 0
    gu_lw = {}
    gu_cons = {}

    def emit_gub(ei2, gcons, gfss, gfvv, gpf):
        E.w("g", "TT", gcons)
        E.w("g", "SS", gfss)
        E.w("g", "VV", gfvv)
        lw = E.dma("LW", "g", lambda e, ei2=ei2: e.dma_start(
            out=Dd["gub_sb"][:], in_=io["gub"][ei2, :, :]))
        E.wt("v", lw)
        E.w("v", "VV", gpf)
        E.v(lambda e: e.tensor_scalar(out=Dd["g7mb"][:], in0=Dd["gub_sb"][:, 0:16],
                                      scalar1=-1.0, scalar2=7.0,
                                      op0=OP.mult, op1=OP.add))
        return E.v(lambda e: e.tensor_scalar_add(Dd["gup1"][:],
                                                 Dd["gub_sb"][:, 16:32], 1.0), "VV")

    gub_vv_next = emit_gub(0, 0, 0, 0, 0)
    for ei in range(NE):
        gub_vv = gub_vv_next
        for j0 in range(2):
            gid0 = (ei * 2) * 16 + j0
            E.w("g", "TT", gu_cons.get(gid0 - 2, 0))
            gu_lw[gid0] = E.dma("LW", "g", lambda e, ei=ei, j0=j0: e.dma_start(
                out=Dd["wgu"][gid0 % 2][:], in_=io["guT"][ei, j0, :, :]))
        E.w("g", "TT", dn_cons_tt)
        dn_lw = E.dma("LW", "g", lambda e, ei=ei: e.dma_start(
            out=Dd["dnT_sb"][:], in_=io["dnT"][ei, :, :]))
        E.w("y", "VV", dnb_cons)
        db = E.dma("LX", "y", lambda e, ei=ei: e.dma_start(
            out=Dd["dnb_st"][:], in_=io["dnb"][0:1, ei * D:(ei + 1) * D]))
        E.wt("v", db)
        E.w("v", "TT", dn_cons_tt)
        vneg = E.v(lambda e: e.tensor_scalar_mul(Dd["dnbn"][0:1, 0:D],
                                                 Dd["dnb_st"][0:1, 0:D], -1.0), "VV")
        dnb_cons = vneg

        for tgh in range(2):
            it = 0
            for j in range(16):
                gid = (ei * 2 + tgh) * 16 + j
                if tgh > 0 or j >= 2:
                    E.w("g", "TT", gu_cons.get(gid - 2, 0))
                    gu_lw[gid] = E.dma("LW", "g", lambda e, ei=ei, j=j, gid=gid: e.dma_start(
                        out=Dd["wgu"][gid % 2][:], in_=io["guT"][ei, j, :, :]))
                for blk in range(2):
                        par = it % 2
                        pg_ap = pA[0][:, 0:512] if par == 0 else pA[2][:, 0:512]
                        pu_ap = pA[1][:, 0:512] if par == 0 else pB[0][:, 0:512]
                        E.wt("t", gu_lw[gid])
                        E.w("t", "SS", pair_ss[par])
                        E.w("t", "VV", pair_vv[par])
                        if it == 0:
                            E.w("t", "VV", gub_vv)
                        if it == 1:
                            E.w("t", "VV", st_vv)
                        xoff = tgh * 1024 + blk * 512
                        for c8 in range(8):
                            E.t(lambda e, c8=c8, gid=gid, pg_ap=pg_ap, xoff=xoff: e.matmul(
                                pg_ap,
                                Dd["wgu"][gid % 2][:, c8 * 128:(c8 + 1) * 128],
                                Dd["x3T"][:, c8 * HALF + xoff:c8 * HALF + xoff + 512],
                                start=(c8 == 0), stop=(c8 == 7), skip_group_check=True))
                        tgu = 0
                        for c8 in range(8):
                            r = E.t(lambda e, c8=c8, gid=gid, pu_ap=pu_ap, xoff=xoff: e.matmul(
                                pu_ap,
                                Dd["wgu"][gid % 2][:, 1024 + c8 * 128:1024 + (c8 + 1) * 128],
                                Dd["x3T"][:, c8 * HALF + xoff:c8 * HALF + xoff + 512],
                                start=(c8 == 0), stop=(c8 == 7), skip_group_check=True),
                                "TT" if c8 == 7 else None)
                            tgu = r or tgu
                        if blk == 1:
                            gu_cons[gid] = tgu
                        so = par * 512
                        E.w("s", "TT", tgu)
                        E.w("s", "VV", act_vv[par])
                        sr7 = E.s(lambda e, so=so, pg_ap=pg_ap, j=j: e.activation(
                            out=Dd["r7"][:, so:so + 512], in_=pg_ap,
                            func=AF.Relu, scale=-1.0,
                            bias=Dd["g7mb"][:, j:j + 1]), "SS")
                        E.w("s", "SS", sr7)
                        sgs = E.s(lambda e, so=so: e.activation(
                            out=Dd["sg"][:, so:so + 512], in_=Dd["r7"][:, so:so + 512],
                            func=AF.Sigmoid, scale=-1.702, bias=1.702 * 7.0), "SS")
                        pair_ss[par] = sgs
                        E.w("v", "SS", sgs)
                        vt1 = E.v(lambda e, so=so: e.scalar_tensor_tensor(
                            out=Dd["t1"][:, so:so + 512], in0=Dd["r7"][:, so:so + 512],
                            scalar=7.0, in1=Dd["sg"][:, so:so + 512],
                            op0=OP.subtract, op1=OP.mult), "VV")
                        E.w("v", "VV", vt1)
                        vuc = E.v(lambda e, so=so, pu_ap=pu_ap, j=j: e.tensor_scalar(
                            out=Dd["r7"][:, so:so + 512], in0=pu_ap,
                            scalar1=Dd["gup1"][:, j:j + 1], scalar2=8.0,
                            op0=OP.add, op1=OP.min), "VV")
                        pair_vv[par] = vuc
                        E.w("v", "VV", vuc)
                        aoff = j * 1024 + blk * 512
                        act_vv[par] = E.v(lambda e, so=so, aoff=aoff: e.scalar_tensor_tensor(
                            out=Dd["actT"][:, aoff:aoff + 512], in0=Dd["r7"][:, so:so + 512],
                            scalar=-6.0, in1=Dd["t1"][:, so:so + 512],
                            op0=OP.max, op1=OP.mult), "VV")
                        it += 1
            act_done = max(act_vv)
            gub_cons_tt = C["TT"].n
            if tgh == 1 and ei + 1 < NE:
                gub_vv_next = emit_gub(ei + 1, gub_cons_tt, max(pair_ss),
                                       max(pair_vv), max(pair_vv))
            E.w("t", "VV", act_done)
            E.wt("t", dn_lw)
            E.w("t", "VV", vneg)
            for t8 in range(8):
                T = tgh * 8 + t8
                pw = pB[t8 % 2]
                E.w("t", "VV", dn_vv[t8 % 2])
                for kc in range(16):
                    for half in range(2):
                        E.t(lambda e, kc=kc, half=half, t8=t8, pw=pw: e.matmul(
                            pw[:, half * 512:(half + 1) * 512],
                            Dd["actT"][:, kc * 1024 + (t8 // 4) * 512 + (t8 % 4) * 128:
                                       kc * 1024 + (t8 // 4) * 512 + (t8 % 4 + 1) * 128],
                            Dd["dnT_sb"][:, kc * D + half * 512:kc * D + (half + 1) * 512],
                            start=(kc == 0), stop=False, skip_group_check=True))
                tdn = 0
                for half in range(2):
                    r = E.t(lambda e, half=half, pw=pw: e.matmul(
                        pw[:, half * 512:(half + 1) * 512], ones_bf[0:1, 0:128],
                        Dd["dnbn"][0:1, half * 512:(half + 1) * 512],
                        start=False, stop=(half == 1), skip_group_check=True),
                        "TT" if half == 1 else None)
                    tdn = r or tdn
                E.w("v", "TT", tdn)
                dn_vv[t8 % 2] = E.v(lambda e, T=T, ei=ei, pw=pw: e.scalar_tensor_tensor(
                    out=resid[:, T * D:(T + 1) * D], in0=pw[:, 0:D],
                    scalar=Dd["ncomb"][:, T * NE + ei:T * NE + ei + 1],
                    in1=resid[:, T * D:(T + 1) * D], op0=OP.mult, op1=OP.add), "VV")
                st_vv = dn_vv[t8 % 2]
            dn_cons_tt = C["TT"].n
        gub_free_ss = max(pair_ss)
        gup1_free = max(pair_vv)
        gub_free_vv = max(pair_vv)


def phaseOut(E, sh, outb, gate):
    C = sh["C"]
    resid, io = sh["resid"], sh["io"]
    ln_gate = {}
    sto = [None, None]
    E.w("v", "VV", gate["VV"])
    for t in range(NT):
        sl = t % 2
        E.wt("v", sto[sl])
        if PHASES >= 4:
            vf = ln_tile(E, sh, resid[:, t * D:(t + 1) * D], outb[:, sl * D:(sl + 1) * D],
                         sl, ln_gate=ln_gate)
        else:
            vf = E.v(lambda e, t=t, sl=sl: e.tensor_copy(
                out=outb[:, sl * D:(sl + 1) * D],
                in_=resid[:, t * D:(t + 1) * D]), "VV")
        E.w("y", "VV", vf)
        sto[sl] = E.dma("STO", "y", lambda e, t=t, sl=sl: e.dma_start(
            out=io["out"][t * 128:(t + 1) * 128, :], in_=outb[:, sl * D:(sl + 1) * D]))


_NC_CACHE = {}


def host_prep(inputs):
    def f32(x):
        return np.ascontiguousarray(np.asarray(x), dtype=np.float32)

    lnv = np.zeros((6, D), np.float32)
    lnv[0] = f32(inputs["L0_attn_ln1_g"]); lnv[1] = f32(inputs["L0_attn_ln1_b"])
    lnv[2] = f32(inputs["L0_attn_ln2_g"]); lnv[3] = f32(inputs["L0_attn_ln2_b"])
    lnv[4] = f32(inputs["L1_attn_ln_gamma"]); lnv[5] = f32(inputs["L1_attn_ln_beta"])
    def ktile(w):
        K, N = w.shape
        return np.ascontiguousarray(
            w.reshape(K // 128, 128, N).transpose(1, 0, 2).reshape(128, -1))

    wiT0 = f32(inputs["L0_ffn_wi_W"]).T
    wiTt = np.ascontiguousarray(
        wiT0.reshape(8, 128, 8, 512).transpose(2, 1, 0, 3).reshape(8, 128, 4096))
    woT0 = f32(inputs["L0_ffn_wo_W"]).T
    woTt = np.ascontiguousarray(
        woT0.reshape(8, 4, 128, D).transpose(0, 2, 1, 3).reshape(8, 128, 4 * D))
    gu0 = f32(inputs["L1_ffn_gate_up_proj"]).transpose(0, 2, 1)
    gu5 = gu0.reshape(NE, 8, 128, 32, 128)
    gt = gu5[:, :, :, :16].transpose(0, 3, 2, 1, 4).reshape(NE, 16, 128, 1024)
    ut = gu5[:, :, :, 16:].transpose(0, 3, 2, 1, 4).reshape(NE, 16, 128, 1024)
    guTt = np.ascontiguousarray(np.concatenate([gt, ut], axis=3))
    dn0 = f32(inputs["L1_ffn_down_proj"]).transpose(0, 2, 1)
    dnTt = np.ascontiguousarray(
        dn0.reshape(NE, 16, 128, D).transpose(0, 2, 1, 3).reshape(NE, 128, 16 * D))
    wo0 = f32(inputs["L0_attn_Wo"])
    wot = np.ascontiguousarray(
        wo0.reshape(16, 64, D).transpose(1, 0, 2).reshape(64, NH * D))
    args = dict(
        lnv=lnv,
        wq=ktile(f32(inputs["L0_attn_Wq"])),
        wkv=ktile(f32(inputs["L0_attn_Wkv"])),
        wo=wot,
        gwT=ktile(np.ascontiguousarray(f32(inputs["L0_ffn_gate_W"]).T)),
        gbv=f32(inputs["L0_ffn_gate_b"]).reshape(1, NE),
        wib=np.ascontiguousarray(f32(inputs["L0_ffn_wi_b"]).reshape(32, 128).T),
        wob=f32(inputs["L0_ffn_wo_b"]).reshape(1, D),
        wiT=wiTt,
        woT=woTt,
        laT=ktile(np.ascontiguousarray(
            f32(inputs["L0_ffn_lora_A"]).transpose(2, 0, 1).reshape(D, 32))),
        lbT=np.ascontiguousarray(f32(inputs["L0_ffn_lora_B"]).transpose(0, 2, 1).reshape(32, FF)),
        w1=np.ascontiguousarray(np.stack([
            ktile(f32(inputs["L1_attn_W_Q"])), ktile(f32(inputs["L1_attn_W_K"])),
            ktile(f32(inputs["L1_attn_W_Kp"])), ktile(f32(inputs["L1_attn_W_V"])),
            ktile(f32(inputs["L1_attn_W_O"]))])),
        rwT=ktile(np.ascontiguousarray(f32(inputs["L1_ffn_router_weight"]).T)),
        rbv=f32(inputs["L1_ffn_router_bias"]).reshape(1, NE),
        guT=guTt,
        gub=np.ascontiguousarray(f32(inputs["L1_ffn_gate_up_proj_bias"]).reshape(NE, 32, 128).transpose(0, 2, 1)),
        dnT=dnTt,
        dnb=f32(inputs["L1_ffn_down_proj_bias"]).reshape(1, -1),
    )
    return args


def kernel(**inputs):
    x = np.ascontiguousarray(np.asarray(inputs["L0_attn_x"]), dtype=np.float32)
    args = host_prep(inputs)

    if "nc" not in _NC_CACHE:
        _NC_CACHE["nc"] = build_nc()
    nc = _NC_CACHE["nc"]

    in_maps = []
    for c in range(8):
        b, h = c // 2, c % 2
        own = x[b, h * HALF:(h + 1) * HALF]
        oth = x[b, (1 - h) * HALF:(2 - h) * HALF]
        xb = np.ascontiguousarray(np.concatenate([own, oth], axis=0))
        rowflag = np.zeros((128, 1), np.float32)
        if h == 1:
            rowflag[96:128] = 1.0
        m = dict(args)
        m["xb"] = xb
        m["rowflag"] = rowflag
        in_maps.append(m)

    res = run_bass_kernel_spmd(nc, in_maps, core_ids=list(range(8)))
    out = np.empty((4, S, D), np.float32)
    for c in range(8):
        b, h = c // 2, c % 2
        out[b, h * HALF:(h + 1) * HALF] = res.results[c]["out"]
    return out


# version 09 (final)
# speedup vs baseline: 1.0327x; 1.0327x over previous
import copy

import numpy as np
from contextlib import ExitStack

import concourse.bass as bass
import concourse.mybir as mybir
from concourse.bass_utils import run_bass_kernel_spmd

F32 = mybir.dt.float32
BF16 = mybir.dt.bfloat16
AF = mybir.ActivationFunctionType
OP = mybir.AluOpType
AX = mybir.AxisListType.X

D = 1024
S = 4096
HALF = 2048
NT = 16
NKT = 32
NH = 16
DH = 64
KC = 32
FF = 4096
NE = 8
GI = 2048
EPS = 1e-5
NQ = 4

PHASES = 4
A_STEPS = 6
KV_STEPS = 4


class Ctr:
    __slots__ = ("sem", "n", "step")

    def __init__(self, sem, step=1):
        self.sem, self.n, self.step = sem, 0, step


LANES = {"LX": ["LX0", "LX1", "LX2", "LX3"],
         "LW": ["LW0", "LW1", "LW2", "LW3", "LW4", "LW5"],
         "STO": ["SO0", "SO1"]}


class Em:
    def __init__(self, k, eng, C):
        self.k, self.e, self.C = k, eng, C

    def dma(self, pool, key, f):
        idx = self.C["_pool_" + pool]
        names = LANES[pool]
        name = names[idx.n % len(names)]
        idx.n += 1
        c = self.C[name]
        self.w(key, name, c.n)
        c.n += 16
        tok = (name, c.n)
        if self.k == key:
            f(self.e).then_inc(c.sem, 16)
        return tok

    def wt(self, key, tok):
        if tok and self.k == key:
            self.e.wait_ge(self.C[tok[0]].sem, tok[1])

    def op(self, key, f, inc=None):
        v = None
        if inc is not None:
            c = self.C[inc]
            c.n += c.step
            v = c.n
        if self.k == key:
            ins = f(self.e)
            if inc is not None:
                ins.then_inc(self.C[inc].sem, self.C[inc].step)
        return v

    def w(self, key, cname, val):
        if self.k == key and val and val > 0:
            self.e.wait_ge(self.C[cname].sem, val)

    def t(self, f, inc=None):
        return self.op("t", f, inc)

    def v(self, f, inc=None):
        return self.op("v", f, inc)

    def s(self, f, inc=None):
        return self.op("s", f, inc)

    def g(self, f, inc=None):
        return self.op("g", f, inc)

    def y(self, f, inc=None):
        return self.op("y", f, inc)


def re_src(src2d, p=128):
    return src2d.rearrange("(t p) d -> p t d", p=p)


def re_dst(dst2d, cols):
    return dst2d.rearrange("p (t d) -> p t d", d=cols)


def ln_tile(E, sh, src_ap, dst_ap, slot, gain=None, bias=None, ln_gate=None,
            pre_v_wait=0, pre_wait=None):
    stat, lnt = sh["stat"], sh["lnt"]
    so = 16 * slot
    if ln_gate is not None and ln_gate.get(slot, 0):
        E.w("v", "SS", ln_gate[slot])
    if pre_v_wait:
        E.w("v", "VV", pre_v_wait)
    if pre_wait:
        E.w("v", pre_wait[0], pre_wait[1])
    E.v(lambda e: e.bn_stats(stat[:, so:so + 6], src_ap[:, 0:512]))
    v0 = E.v(lambda e: e.bn_stats(stat[:, so + 6:so + 12], src_ap[:, 512:1024]), "VV")
    E.w("v", "VV", v0)
    v1 = E.v(lambda e: e.bn_aggr(stat[:, so + 12:so + 14], stat[:, so:so + 12]), "VV")
    E.w("v", "VV", v1)
    E.v(lambda e: e.tensor_scalar_mul(stat[:, so + 14:so + 15], stat[:, so + 12:so + 13], -1.0))
    v2 = E.v(lambda e: e.tensor_scalar_add(stat[:, so + 15:so + 16], stat[:, so + 13:so + 14], EPS), "VV")
    E.w("s", "VV", v2)
    s0 = E.s(lambda e: e.activation(out=stat[:, so + 13:so + 14], in_=stat[:, so + 15:so + 16], func=AF.Ln), "SS")
    E.w("s", "SS", s0)
    s1 = E.s(lambda e: e.activation(out=stat[:, so + 15:so + 16], in_=stat[:, so + 13:so + 14],
                                    func=AF.Exp, scale=-0.5), "SS")
    if ln_gate is not None:
        ln_gate[slot] = s1
    E.w("v", "SS", s1)
    if gain is None:
        return E.v(lambda e: e.tensor_scalar(out=dst_ap, in0=src_ap,
                                             scalar1=stat[:, so + 14:so + 15],
                                             scalar2=stat[:, so + 15:so + 16],
                                             op0=OP.add, op1=OP.mult), "VV")
    v3 = E.v(lambda e: e.tensor_scalar(out=lnt[:, 0:D], in0=src_ap,
                                       scalar1=stat[:, so + 14:so + 15],
                                       scalar2=stat[:, so + 15:so + 16],
                                       op0=OP.add, op1=OP.mult), "VV")
    E.w("v", "VV", v3)
    v4 = E.v(lambda e: e.tensor_mul(lnt[:, 0:D], lnt[:, 0:D], gain), "VV")
    E.w("v", "VV", v4)
    return E.v(lambda e: e.tensor_add(dst_ap, lnt[:, 0:D], bias), "VV")


def phase_gate(E, gate, keys=("g", "y")):
    for key in keys:
        for k2, v in gate.items():
            E.w(key, k2, v)


def bcast_vec(E, sh, dram_row_ap, dst_sb, pb):
    C = sh["C"]
    ones, stg1 = sh["ones"], sh["stg1"]
    E.w("y", "TT", sh.get("stg1_tt", 0))
    lx = E.dma("LX", "y", lambda e: e.dma_start(out=stg1[0:1, 0:D], in_=dram_row_ap))
    E.wt("t", lx)
    tv = 0
    for j in range(2):
        tv = E.t(lambda e, j=j: e.matmul(pb[:, j * 512:(j + 1) * 512],
                                         ones[0:1, 0:128], stg1[0:1, j * 512:(j + 1) * 512],
                                         start=True, stop=True, skip_group_check=True), "TT")
    sh["stg1_tt"] = tv
    E.w("s", "TT", tv)
    return E.s(lambda e: e.copy(out=dst_sb[:, 0:D], in_=pb[:, 0:D]), "SS")


def rank_counts(E, sh, dd, lgt_psum):
    E.v(lambda e: e.memset(dd["cA"][:], 0.0))
    E.v(lambda e: e.memset(dd["cB"][:], 0.0))
    vc = E.v(lambda e: e.tensor_copy(out=dd["lgt"][:], in_=lgt_psum), "VV")
    E.w("v", "VV", vc)
    lgt_ap = dd["lgt"][:]
    for d_ in range(1, NE):
        E.v(lambda e, d_=d_: e.tensor_tensor(
            out=dd["sA"][:, 0:NE - d_], in0=lgt_ap[:, d_:NE],
            in1=lgt_ap[:, 0:NE - d_], op=OP.is_gt))
        v1 = E.v(lambda e, d_=d_: e.tensor_tensor(
            out=dd["sB"][:, d_:NE], in0=lgt_ap[:, 0:NE - d_],
            in1=lgt_ap[:, d_:NE], op=OP.is_ge), "VV")
        E.w("v", "VV", v1)
        E.v(lambda e, d_=d_: e.tensor_add(dd["cA"][:, 0:NE - d_],
                                          dd["cA"][:, 0:NE - d_], dd["sA"][:, 0:NE - d_]))
        v2 = E.v(lambda e, d_=d_: e.tensor_add(dd["cB"][:, d_:NE],
                                               dd["cB"][:, d_:NE], dd["sB"][:, d_:NE]), "VV")
        E.w("v", "VV", v2)
    vf = E.v(lambda e: e.tensor_add(dd["cA"][:], dd["cA"][:], dd["cB"][:]), "VV")
    E.w("v", "VV", vf)
    return vf


def build_nc():
    nc = bass.Bass("TRN2", target_bir_lowering=False)
    io = {}

    def din(name, shape, dt=F32):
        io[name] = nc.dram_tensor(name, shape, dt, kind="ExternalInput").ap()

    din("xb", [S, D])
    din("rowflag", [128, 1])
    din("lnv", [6, D])
    din("wq", [128, 8 * D]); din("wkv", [128, 8 * 128]); din("wo", [64, NH * D])
    din("gwT", [128, 8 * NE]); din("gbv", [1, NE])
    din("wib", [128, 32]); din("wob", [1, D])
    din("wiT", [8, 128, 8 * 512]); din("woT", [8, 128, 4 * D])
    din("laT", [128, 8 * 32]); din("lbT", [32, FF])
    din("w1", [5, 128, 8 * D])
    din("rwT", [128, 8 * NE]); din("rbv", [1, NE])
    din("guT", [NE, 16, 128, 2048]); din("gub", [NE, 128, 32])
    din("dnT", [NE, 128, 16 * D]); din("dnb", [1, NE * D])
    io["out"] = nc.dram_tensor("out", [HALF, D], F32, kind="ExternalOutput").ap()

    with ExitStack() as STK:
        sb = lambda n_, s_, d_: STK.enter_context(nc.sbuf_tensor(n_, s_, d_))
        ps = lambda n_, s_, d_: STK.enter_context(nc.psum_tensor(n_, s_, d_))

        resid = sb("resid", [128, NT * D], F32)
        idf = sb("idf", [128, 128], F32)
        idb = sb("idb", [128, 128], BF16)
        ones = sb("ones", [128, 128], F32)
        ones_bf = sb("ones_bf", [1, 512], BF16)
        cvals = sb("cvals", [128, 2], F32)
        rflag = sb("rflag", [128, 1], F32)
        stat = sb("stat", [128, 32], F32)
        lnt = sb("lnt", [128, D], F32)
        stg1 = sb("stg1", [1, D], F32)

        pA = [ps(f"pA{i}", [128, 512], F32) for i in range(3)]
        pB = [ps(f"pB{i}", [128, 1024], F32) for i in range(2)]
        pF = ps("pF", [128, 1024], BF16)

        semnames = (["TT", "VV", "SS", "GG"] + LANES["LX"] + LANES["LW"]
                    + LANES["STO"])
        sems = {k: STK.enter_context(nc.semaphore(k)) for k in semnames}
        C = {k: Ctr(sems[k], 16 if k[:2] in ("LX", "LW", "SO") else 1)
             for k in semnames}
        for pool in LANES:
            C["_pool_" + pool] = Ctr(None, 1)

        nc.const_aps.register(F32, 7.0, cvals[:, 0:1])
        nc.const_aps.register(F32, 1.702 * 7.0, cvals[:, 1:2])

        block = STK.enter_context(nc.Block())
        ENGS = [("y", "sync"), ("g", "gpsimd"), ("t", "tensor"),
                ("v", "vector"), ("s", "scalar")]

        def run_phase(body, state=None):
            snap = {k2: c.n for k2, c in C.items()}
            st_snap = copy.deepcopy(state) if state is not None else None
            finals = None
            fin_state = None
            for idx, (key, attr) in enumerate(ENGS):
                for k2, c in C.items():
                    c.n = snap[k2]
                if state is not None and idx > 0:
                    fin_state = copy.deepcopy(state)
                    state.clear()
                    state.update(copy.deepcopy(st_snap))
                def f(eng, key=key):
                    body(Em(key, eng, C))
                getattr(block, attr)(f)
                cur = {k2: c.n for k2, c in C.items()}
                if finals is None:
                    finals = cur
                else:
                    assert cur == finals, f"counter divergence {key}: {cur} vs {finals}"
                if state is not None and fin_state is not None:
                    assert state == fin_state, f"state divergence on {key}"

        sh = dict(nc=nc, io=io, C=C, resid=resid, idf=idf, idb=idb, ones=ones,
                  ones_bf=ones_bf, cvals=cvals, rflag=rflag,
                  stat=stat, lnt=lnt, stg1=stg1, pA=pA, pB=pB, pF=pF)

        def init_body(E):
            E.v(lambda e: e.memset(ones[:], 1.0))
            E.v(lambda e: e.memset(ones_bf[:], 1.0))
            E.v(lambda e: e.memset(cvals[:, 0:1], 7.0))
            E.v(lambda e: e.memset(cvals[:, 1:2], 1.702 * 7.0), "VV")
            g1 = E.g(lambda e: e.memset(idf[:], 1.0), "GG")
            E.w("g", "GG", g1)
            E.g(lambda e: e.affine_select(out=idf[:], in_=idf[:], pattern=[[1, 128]],
                                          base=0, channel_multiplier=-1,
                                          compare_op=OP.is_equal, fill=0.0))
            g2 = E.g(lambda e: e.memset(idb[:], 1.0), "GG")
            E.w("g", "GG", g2)
            E.g(lambda e: e.affine_select(out=idb[:], in_=idb[:], pattern=[[1, 128]],
                                          base=0, channel_multiplier=-1,
                                          compare_op=OP.is_equal, fill=0.0), "GG")
            tok = E.dma("LX", "y", lambda e: e.dma_start(out=rflag[:], in_=io["rowflag"][:]))
            init_body.tok = tok
            for key in ("t", "v", "s", "g", "y"):
                E.w(key, "VV", C["VV"].n)
                E.w(key, "GG", C["GG"].n)
        run_phase(init_body)
        init_lx = ("LX0", 16)

        with ExitStack() as SA:
            sba = lambda n_, s_, d_: SA.enter_context(nc.sbuf_tensor(n_, s_, d_))
            A = dict(
                gbc=sba("gbcA", [128, D], F32),
                bbc=sba("bbcA", [128, D], F32),
                wbuf=sba("wbuf", [128, NH * D], BF16),
                wkv_sb=sba("wkv_sb", [128, 8 * 128], BF16),
                kT=sba("kT", [128, S], BF16),
                vones=sba("vones", [128, KC * 128], BF16),
                qhT=sba("qhT", [128, NH * 512], BF16),
                oaT=sba("oaT", [128, NH * 512], BF16),
                gbc2=sba("gbc2", [128, D], F32),
                bbc2=sba("bbc2", [128, D], F32),
                xst=sba("xst", [128, 2 * D], F32),
                xnb=sba("xnb", [128, 2 * D], BF16),
                xnT=sba("xnT", [128, 2 * D], BF16),
                qtm=sba("qtm", [128, 2 * D], BF16),
                q2a=sba("q2a", [128, NQ * NH], F32),
                sqs=sba("sqs", [128, D], F32),
                pTs=sba("pTs", [128, 3 * 1024], BF16),
                und=sba("und", [64, 512], BF16),
                rrec=sba("rrec", [65, 512], F32),
                sc1=sba("sc1", [128, 8], F32),
                cbias=sba("cbias", [128, 1], F32),
            )
            st = dict(ln_gate={}, kv_ss=0, pf_evac=0, xnb_gate=[0, 0],
                      xnT_gate=[0, 0], qtm_gate=[0, 0], q_evac_ss=[0, 0],
                      cb_ss=0, oa_gate_tt=0, attn_done_tt=0, wo_done_tt=0,
                      grp_free_vv=[0, 0], last_exp_slot=[0, 0, 0],
                      last_pv_slot=[0, 0, 0], sbcast=0, sb2=0, q_done_vv=0)
            run_phase(lambda E: phA_kv_all(E, sh, A, st, init_lx), state=st)
            nq = 4 if A_STEPS >= 6 else 1
            for qq in range(nq):
                if A_STEPS < 2:
                    break
                run_phase(lambda E, qq=qq: phA_q(E, sh, A, st, qq), state=st)
                if qq == 0 and A_STEPS >= 3:
                    run_phase(lambda E: phA_c(E, sh, A, st), state=st)
                if A_STEPS >= 4:
                    run_phase(lambda E, qq=qq: phA_attn(E, sh, A, st, qq), state=st)
                if A_STEPS >= 5:
                    run_phase(lambda E, qq=qq: phA_wo(E, sh, A, st, qq), state=st)
        GATE_KEYS = ["TT", "VV", "SS"] + LANES["LX"] + LANES["LW"] + LANES["STO"]
        gateA = {k2: C[k2].n for k2 in GATE_KEYS}

        if PHASES >= 2:
            with ExitStack() as SB:
                sbb = lambda n_, s_, d_: SB.enter_context(nc.sbuf_tensor(n_, s_, d_))
                Bd = dict(
                    gwT_sb=sbb("gwT_sb", [128, 8 * NE], BF16),
                    laT_sb=sbb("laT_sb", [128, 8 * 32], BF16),
                    lbT_sb=sbb("lbT_sb", [32, FF], BF16),
                    wib_sb=sbb("wib_sb", [128, 32], F32),
                    wob_sb=sbb("wob_sb", [1, D], BF16),
                    gb_sb=sbb("gb_sb", [1, NE], F32),
                    wibuf=sbb("wibuf", [128, 2 * 4096], BF16),
                    wobuf=sbb("wobuf", [128, 2 * 4096], BF16),
                    x1T=sbb("x1T", [128, 8 * 512], BF16),
                    interT=sbb("interT", [128, 32 * 512], BF16),
                    rselT=sbb("rselT", [32, 512], BF16),
                    lgt=sbb("lgtB", [128, NE], F32),
                    sA=sbb("sAB", [128, NE], F32),
                    sB=sbb("sBB", [128, NE], F32),
                    cA=sbb("cAB", [128, NE], F32),
                    cB=sbb("cBB", [128, NE], F32),
                    msk=sbb("mskB", [128, NE], F32),
                    rsel=sbb("rselB", [128, 32], F32),
                    ytmp=sbb("ytmpB", [128, 2 * D], F32),
                )
                run_phase(lambda E: phaseB(E, sh, Bd, gateA))
            gateB = {k2: C[k2].n for k2 in GATE_KEYS}
        else:
            gateB = gateA

        if PHASES >= 3:
            with ExitStack() as SC:
                sbc = lambda n_, s_, d_: SC.enter_context(nc.sbuf_tensor(n_, s_, d_))
                Cd = dict(
                    gbc=sbc("gbcC", [128, D], F32),
                    bbc=sbc("bbcC", [128, D], F32),
                    w_sb=[sbc(f"w1_{i}", [128, 8 * D], BF16) for i in range(2)],
                    xT15=sbc("xT15", [128, D], BF16),
                    tm=[sbc(f"tmC_{i}", [128, D], F32) for i in range(4)],
                    bm=[sbc(f"bmC_{i}", [128, D], BF16) for i in range(4)],
                    fm=[sbc(f"fmC_{i}", [64, 128], BF16) for i in range(4)],
                    fmf=[sbc(f"fmfC_{i}", [64, 128], F32) for i in range(2)],
                    Mh=sbc("Mh", [64, 1024], BF16),
                    Nh=sbc("Nh", [64, 1024], BF16),
                    Ntm=sbc("Ntm", [128, 512], BF16),
                    Af=sbc("Af", [128, 1024], F32),
                    Pf=sbc("Pf", [128, 1024], F32),
                    Pb=sbc("Pb", [128, 1024], BF16),
                    pT=sbc("pTc", [128, 8 * 128], BF16),
                    Ztm=sbc("Ztm", [128, D], BF16),
                    ZT=sbc("ZT", [128, 8 * 128], BF16),
                    zmask=sbc("zmask", [128, D], F32),
                    tin=sbc("tinC", [128, D], F32),
                    att=sbc("attC", [128, D], F32),
                    ssq=sbc("ssqC", [128, 4 * NH], F32),
                    smx=sbc("smxC", [128, 4], F32),
                    wcons={},
                )
                run_phase(lambda E: phaseC(E, sh, Cd, gateB))
            gateC = {k2: C[k2].n for k2 in GATE_KEYS}
        else:
            gateC = gateB

        if PHASES >= 4:
            with ExitStack() as SD:
                sbd = lambda n_, s_, d_: SD.enter_context(nc.sbuf_tensor(n_, s_, d_))
                Dd = dict(
                    x3T=sbd("x3T", [128, 8 * HALF], BF16),
                    rwT_sb=sbd("rwT_sb", [128, 8 * NE], BF16),
                    rb_sb=sbd("rb_sb", [1, NE], F32),
                    gub_sb=sbd("gub_sb", [128, 32], F32),
                    g7mb=sbd("g7mb", [128, 16], F32),
                    gup1=sbd("gup1", [128, 16], F32),
                    dnb_st=sbd("dnb_st", [1, D], F32),
                    dnbn=sbd("dnbn", [1, D], BF16),
                    ncomb=sbd("ncomb", [128, NT * NE], F32),
                    lgt=sbd("lgtD", [128, NE], F32),
                    sA=sbd("sAD", [128, NE], F32),
                    sB=sbd("sBD", [128, NE], F32),
                    cA=sbd("cAD", [128, NE], F32),
                    cB=sbd("cBD", [128, NE], F32),
                    sel=sbd("selD", [128, NE], F32),
                    pex=sbd("pexD", [128, NE], F32),
                    nmx=sbd("nmxD", [128, 2], F32),
                    wgu=[sbd(f"wgu_{i}", [128, 2048], BF16) for i in range(2)],
                    actT=sbd("actT", [128, 16 * 1024], BF16),
                    dnT_sb=sbd("dnTsb", [128, 16 * 1024], BF16),
                    r7=sbd("r7D", [128, 2 * 512], F32),
                    sg=sbd("sgD", [128, 2 * 512], F32),
                    t1=sbd("t1D", [128, 2 * 512], F32),
                )
                run_phase(lambda E: phaseD(E, sh, Dd, gateC))
            gateD = {k2: C[k2].n for k2 in GATE_KEYS}
        else:
            gateD = gateC

        with ExitStack() as SE:
            outb = SE.enter_context(nc.sbuf_tensor("outb", [128, 2 * D], F32))
            run_phase(lambda E: phaseOut(E, sh, outb, gateD))

        def fin(E):
            for name in LANES["STO"]:
                E.w("y", name, C[name].n)
        run_phase(fin)

    return nc


def phA_kv_all(E, sh, A, st, init_lx):
    io, C = sh["io"], sh["C"]
    resid, ones, idb = sh["resid"], sh["ones"], sh["idb"]
    pA, pB, pF = sh["pA"], sh["pB"], sh["pF"]
    gbc, bbc = A["gbc"], A["bbc"]

    if KV_STEPS >= 1:
        wz = E.v(lambda e: e.memset(A["wbuf"][64:128, :], 0.0), "VV")
        E.w("g", "VV", wz)
        st["lw_wq"] = E.dma("LW", "g", lambda e: e.dma_start(
            out=A["wbuf"][:, 0:8 * D], in_=io["wq"][:, :]))
        st["lw_wkv"] = E.dma("LW", "g", lambda e: e.dma_start(
            out=A["wkv_sb"][:], in_=io["wkv"][:, :]))

        E.wt("t", init_lx)
        s1 = bcast_vec(E, sh, io["lnv"][0:1, :], gbc, pB[0])
        st["sbcast"] = bcast_vec(E, sh, io["lnv"][1:2, :], bbc, pB[1])
        vz = E.v(lambda e: e.memset(A["vones"][:], 0.0), "VV")
        E.w("v", "VV", vz)
        E.v(lambda e: e.memset(
            A["vones"][:, :].rearrange("p (k c) -> p k c", c=128)[:, :, 64:65], 1.0))
        E.v(lambda e: e.memset(A["kT"][64:128, :], 0.0))
        E.v(lambda e: e.memset(A["oaT"][64:128, :], 0.0))
        vset = E.v(lambda e: e.memset(A["qhT"][64:128, :], 0.0), "VV")
        st["pad_zero_vv"] = vset

    ownld = []
    for g4 in range(4):
        v = E.dma("LX", "y", lambda e, g4=g4: e.dma_start(
            out=re_dst(resid[:, g4 * 4 * D:(g4 + 1) * 4 * D], D),
            in_=re_src(io["xb"][g4 * 512:(g4 + 1) * 512, :])))
        ownld.append(v)

    stage_vv = [0] * 16
    othld = [0] * 16

    for kt in range(NKT):
        own = kt < NT
        sl = kt % 2
        if own:
            E.wt("v", ownld[kt // 4])
            src = resid[:, kt * D:(kt + 1) * D]
        else:
            if KV_STEPS < 1:
                continue
            gi = kt - NT
            if gi >= 2:
                E.w("y", "VV", stage_vv[gi - 2])
            othld[gi] = E.dma("LX", "y", lambda e, gi=gi: e.dma_start(
                out=A["xst"][:, (gi % 2) * D:((gi % 2) + 1) * D],
                in_=io["xb"][NT * 128 + gi * 128: NT * 128 + (gi + 1) * 128, :]))
            E.wt("v", othld[gi])
            src = A["xst"][:, (gi % 2) * D:((gi % 2) + 1) * D]
        if kt == 0:
            E.w("v", "SS", st["sbcast"])

        if KV_STEPS < 2:
            if not own:
                stage_vv[kt - NT] = 1
            continue
        xnb_ap = A["xnb"][:, sl * D:(sl + 1) * D]
        vln = ln_tile(E, sh, src, xnb_ap, sl, gain=gbc[:, 0:D], bias=bbc[:, 0:D],
                      ln_gate=st["ln_gate"], pre_wait=("TT", st["xnb_gate"][sl]))
        if not own:
            stage_vv[kt - NT] = vln
        if KV_STEPS < 3:
            continue

        E.w("t", "VV", vln)
        E.w("t", "SS", st["pf_evac"])
        ttr = 0
        for c8 in range(8):
            r = E.t(lambda e, c8=c8, xnb_ap=xnb_ap: e.transpose(
                pF[:, c8 * 128:(c8 + 1) * 128], xnb_ap[:, c8 * 128:(c8 + 1) * 128], idb[:]),
                "TT" if c8 == 7 else None)
            ttr = r or ttr
        st["xnb_gate"][sl] = ttr
        xnT_ap = A["xnT"][:, sl * D:(sl + 1) * D]
        E.w("s", "TT", ttr)
        E.w("s", "TT", st["xnT_gate"][sl])
        sevac = E.s(lambda e, xnT_ap=xnT_ap: e.copy(out=xnT_ap, in_=pF[:, 0:D]), "SS")
        st["pf_evac"] = sevac
        if KV_STEPS < 4:
            continue

        E.w("t", "SS", sevac)
        if kt == 0:
            E.wt("t", st["lw_wkv"])
            E.w("t", "VV", vset)
        else:
            E.w("t", "SS", st["kv_ss"])
        tkt = 0
        for c8 in range(8):
            r = E.t(lambda e, c8=c8, xnT_ap=xnT_ap: e.matmul(
                pA[1][0:64, 0:128], A["wkv_sb"][:, c8 * 128:c8 * 128 + 64],
                xnT_ap[:, c8 * 128:(c8 + 1) * 128],
                start=(c8 == 0), stop=(c8 == 7), skip_group_check=True),
                "TT" if c8 == 7 else None)
            tkt = r or tkt
        tvt = 0
        for c8 in range(8):
            r = E.t(lambda e, c8=c8, xnT_ap=xnT_ap: e.matmul(
                pA[2][:, 0:64], xnT_ap[:, c8 * 128:(c8 + 1) * 128],
                A["wkv_sb"][:, c8 * 128 + 64:c8 * 128 + 128],
                start=(c8 == 0), stop=(c8 == 7), skip_group_check=True),
                "TT" if c8 == 7 else None)
            tvt = r or tvt
        st["xnT_gate"][sl] = tvt

        E.w("s", "TT", tkt)
        E.s(lambda e, kt=kt: e.copy(out=A["kT"][0:64, kt * 128:(kt + 1) * 128],
                                    in_=pA[1][0:64, 0:128]))
        E.w("s", "TT", tvt)
        st["kv_ss"] = E.s(lambda e, kt=kt: e.copy(
            out=A["vones"][:, kt * 128:kt * 128 + 64], in_=pA[2][:, 0:64]), "SS")
    st["kv_last_ss"] = st["kv_ss"]


def phA_q(E, sh, A, st, qq):
    io, C = sh["io"], sh["C"]
    resid, idb = sh["resid"], sh["idb"]
    pA, pB, pF = sh["pA"], sh["pB"], sh["pF"]
    gbc, bbc = A["gbc"], A["bbc"]

    if qq > 0:
        E.w("g", "TT", st["wo_done_tt"])
        st["lw_wq"] = E.dma("LW", "g", lambda e: e.dma_start(
            out=A["wbuf"][:, 0:8 * D], in_=io["wq"][:, :]))

    for i4 in range(NQ):
        kt = qq * NQ + i4
        sl = i4 % 2
        src = resid[:, kt * D:(kt + 1) * D]
        xnb_ap = A["xnb"][:, sl * D:(sl + 1) * D]
        vln = ln_tile(E, sh, src, xnb_ap, sl, gain=gbc[:, 0:D], bias=bbc[:, 0:D],
                      ln_gate=st["ln_gate"], pre_wait=("TT", st["xnb_gate"][sl]))
        E.w("t", "VV", vln)
        E.w("t", "SS", st["pf_evac"])
        ttr = 0
        for c8 in range(8):
            r = E.t(lambda e, c8=c8, xnb_ap=xnb_ap: e.transpose(
                pF[:, c8 * 128:(c8 + 1) * 128], xnb_ap[:, c8 * 128:(c8 + 1) * 128], idb[:]),
                "TT" if c8 == 7 else None)
            ttr = r or ttr
        st["xnb_gate"][sl] = ttr
        xnT_ap = A["xnT"][:, sl * D:(sl + 1) * D]
        E.w("s", "TT", ttr)
        E.w("s", "TT", st["xnT_gate"][sl])
        if qq > 0 and i4 == 0:
            E.w("s", "TT", st["attn_done_tt"])
        sevac = E.s(lambda e, xnT_ap=xnT_ap: e.copy(out=xnT_ap, in_=pF[:, 0:D]), "SS")
        st["pf_evac"] = sevac

        E.w("t", "SS", sevac)
        if i4 == 0:
            E.wt("t", st["lw_wq"])
            E.w("t", "SS", st["sbcast"])
            E.w("t", "VV", st["grp_free_vv"][0])
            E.w("t", "VV", st["grp_free_vv"][1])
        E.w("t", "SS", st["q_evac_ss"][i4 % 2])
        tq = 0
        for half in range(2):
            for c8 in range(8):
                r = E.t(lambda e, c8=c8, half=half, i4=i4, xnT_ap=xnT_ap: e.matmul(
                    pB[i4 % 2][:, half * 512:(half + 1) * 512],
                    xnT_ap[:, c8 * 128:(c8 + 1) * 128],
                    A["wbuf"][:, c8 * D + half * 512:c8 * D + (half + 1) * 512],
                    start=(c8 == 0), stop=(c8 == 7), skip_group_check=True),
                    "TT" if (half == 1 and c8 == 7) else None)
                tq = r or tq
        st["xnT_gate"][sl] = tq
        E.w("s", "TT", tq)
        E.w("s", "TT", st["qtm_gate"][sl])
        qtm_ap = A["qtm"][:, sl * D:(sl + 1) * D]
        st["q_evac_ss"][i4 % 2] = E.s(lambda e, i4=i4, qtm_ap=qtm_ap: e.mul(
            out=qtm_ap, in_=pB[i4 % 2][:, 0:D], mul=0.125), "SS")
        sqv = st["q_evac_ss"][i4 % 2]

        if qq == 0:
            E.w("v", "SS", sqv)
            vsq = E.v(lambda e, qtm_ap=qtm_ap: e.tensor_mul(A["sqs"][:, 0:D], qtm_ap, qtm_ap), "VV")
            E.w("v", "VV", vsq)
            E.v(lambda e, i4=i4: e.tensor_reduce(
                A["q2a"][:, i4 * NH:(i4 + 1) * NH],
                A["sqs"][:, 0:D].rearrange("p (h dd) -> p h dd", h=NH),
                axis=AX, op=OP.add), "VV")
            st["q_done_vv"] = C["VV"].n

        for rnd in range(2):
            E.w("t", "SS", st["pf_evac"])
            E.w("t", "SS", sqv)
            trh = 0
            for hh in range(8):
                h = rnd * 8 + hh
                r = E.t(lambda e, h=h, hh=hh, qtm_ap=qtm_ap: e.transpose(
                    pF[0:64, hh * 128:hh * 128 + 128], qtm_ap[:, h * DH:(h + 1) * DH], idb[:]),
                    "TT" if hh == 7 else None)
                trh = r or trh
            E.w("s", "TT", trh)
            pe = 0
            for hh in range(8):
                h = rnd * 8 + hh
                r = E.s(lambda e, h=h, hh=hh, i4=i4: e.copy(
                    out=A["qhT"][0:64, h * 512 + i4 * 128:h * 512 + (i4 + 1) * 128],
                    in_=pF[0:64, hh * 128:hh * 128 + 128]),
                    "SS" if hh == 7 else None)
                pe = r or pe
            st["pf_evac"] = pe
        st["qtm_gate"][sl] = C["TT"].n


def phA_c(E, sh, A, st):
    C = sh["C"]
    pA, ones, idf = sh["pA"], sh["ones"], sh["idf"]

    E.w("v", "SS", st["kv_last_ss"])
    E.w("v", "VV", st["q_done_vv"])
    lastmx = 0
    for piece in range(4):
        if piece > 0:
            E.w("v", "VV", lastmx)
        vv = E.v(lambda e, piece=piece: e.tensor_mul(
            A["sqs"][0:64, 0:D], A["kT"][0:64, piece * D:(piece + 1) * D],
            A["kT"][0:64, piece * D:(piece + 1) * D]), "VV")
        E.w("t", "VV", vv)
        for j in range(2):
            tv = E.t(lambda e, j=j: e.matmul(
                pA[0][0:1, 0:512], ones[0:64, 0:1],
                A["sqs"][0:64, j * 512:(j + 1) * 512],
                start=True, stop=True, skip_group_check=True), "TT")
            E.w("v", "TT", tv)
            vmx = E.v(lambda e: e.reduce_max(A["sc1"][0:1, 4:5], pA[0][0:1, 0:512],
                                             axis=AX), "VV")
            E.w("v", "VV", vmx)
            if piece == 0 and j == 0:
                lastmx = E.v(lambda e: e.tensor_copy(out=A["sc1"][0:1, 0:1],
                                                     in_=A["sc1"][0:1, 4:5]), "VV")
            else:
                lastmx = E.v(lambda e: e.tensor_max(A["sc1"][0:1, 0:1], A["sc1"][0:1, 0:1],
                                                    A["sc1"][0:1, 4:5]), "VV")
            E.w("t", "VV", lastmx)

    E.w("v", "VV", st["q_done_vv"])
    v2 = E.v(lambda e: e.reduce_max(A["sc1"][:, 1:2], A["q2a"][:, 0:NQ * NH], axis=AX), "VV")
    E.w("t", "VV", v2)
    E.w("t", "SS", st["kv_last_ss"])
    t2 = E.t(lambda e: e.transpose(pA[1][0:1, 0:128], A["sc1"][:, 1:2], idf[:]), "TT")
    E.w("v", "TT", t2)
    v3 = E.v(lambda e: e.reduce_max(A["sc1"][0:1, 2:3], pA[1][0:1, 0:128], axis=AX), "VV")
    E.w("v", "VV", max(v3, lastmx))
    v4 = E.v(lambda e: e.tensor_mul(A["sc1"][0:1, 3:4], A["sc1"][0:1, 0:1],
                                    A["sc1"][0:1, 2:3]), "VV")
    E.w("s", "VV", v4)
    sc0 = E.s(lambda e: e.activation(out=A["sc1"][0:1, 4:5], in_=A["sc1"][0:1, 3:4], func=AF.Ln), "SS")
    E.w("s", "SS", sc0)
    sc = E.s(lambda e: e.activation(out=A["sc1"][0:1, 5:6], in_=A["sc1"][0:1, 4:5],
                                    func=AF.Exp, scale=0.5), "SS")
    E.w("v", "SS", sc)
    v5 = E.v(lambda e: e.tensor_scalar_mul(A["sc1"][0:1, 6:7], A["sc1"][0:1, 5:6], -1.0), "VV")
    E.w("t", "VV", v5)
    t3 = E.t(lambda e: e.matmul(pA[1][:, 0:1], ones[0:1, 0:128], A["sc1"][0:1, 6:7],
                                start=True, stop=True, skip_group_check=True), "TT")
    E.w("s", "TT", t3)
    st["cb_ss"] = E.s(lambda e: e.copy(out=A["cbias"][:], in_=pA[1][:, 0:1]), "SS")


def phA_attn(E, sh, A, st, qq):
    C = sh["C"]
    pA, pB = sh["pA"], sh["pB"]
    ones = sh["ones"]

    les, lps = st["last_exp_slot"], st["last_pv_slot"]
    gfv = st["grp_free_vv"]
    NP = KC // 2

    def emit_qk(h, kp, slot):
        qoff = h * 512
        E.w("t", "SS", les[slot])
        for sub in range(2):
            kc = 2 * kp + sub
            r = E.t(lambda e, kc=kc, sub=sub, slot=slot, qoff=qoff: e.matmul(
                pB[slot][:, sub * 512:(sub + 1) * 512],
                A["kT"][:, kc * 128:(kc + 1) * 128],
                A["qhT"][:, qoff:qoff + 512],
                start=True, stop=True, skip_group_check=True),
                "TT" if sub == 1 else None)
        return r

    def emit_exp(qkv, slot, pslot):
        E.w("s", "TT", qkv)
        E.w("s", "TT", lps[pslot])
        les[slot] = E.s(lambda e, slot=slot, pslot=pslot: e.activation(
            out=A["pTs"][:, pslot * 1024:(pslot + 1) * 1024],
            in_=pB[slot][:, 0:1024], func=AF.Exp,
            bias=A["cbias"][:, 0:1], scale=1.0), "SS")
        return les[slot]

    def emit_pv(h, kp, pslot, ev):
        pva = pA[h % 2]
        E.w("t", "SS", ev)
        for sub in range(2):
            kc = 2 * kp + sub
            r = E.t(lambda e, kc=kc, sub=sub, pslot=pslot, pva=pva: e.matmul(
                pva[:, 0:512], A["vones"][:, kc * 128:kc * 128 + 128],
                A["pTs"][:, pslot * 1024 + sub * 512:pslot * 1024 + (sub + 1) * 512],
                start=(kc == 0), stop=(kc == KC - 1), skip_group_check=True),
                "TT" if sub == 1 else None)
        lps[pslot] = r
        return r

    def epilogue(h, pvd):
        pva = pA[h % 2]
        qoff = h * 512
        E.w("v", "TT", pvd)
        E.w("v", "VV", st.get("rrec_free", 0))
        vr = E.v(lambda e, pva=pva: e.reciprocal(A["rrec"][64:65, 0:512],
                                                 pva[64:65, 0:512]), "VV")
        E.w("t", "VV", vr)
        E.w("t", "VV", st.get("bc_free_vv", 0))
        tb = E.t(lambda e: e.matmul(
            pA[2][0:64, 0:512], ones[64:65, 0:64], A["rrec"][64:65, 0:512],
            start=True, stop=True, skip_group_check=True,
            tile_position=(64, 0)), "TT")
        E.w("s", "TT", pvd)
        E.w("s", "VV", gfv[1 - h % 2])
        su = E.s(lambda e, pva=pva: e.copy(out=A["und"][0:64, 0:512],
                                           in_=pva[0:64, 0:512]), "SS")
        E.w("v", "TT", tb)
        E.w("v", "SS", su)
        gfv[h % 2] = E.v(lambda e, qoff=qoff: e.tensor_mul(
            A["oaT"][0:64, qoff:qoff + 512], A["und"][0:64, 0:512],
            pA[2][0:64, 0:512]), "VV")
        st["bc_free_vv"] = gfv[h % 2]
        st["rrec_free"] = vr

    pending = None
    for h in range(NH):
        pva = pA[h % 2]
        E.w("t", "VV", gfv[h % 2])
        if h < 2:
            E.w("t", "SS", st["q_evac_ss"][0])
            E.w("t", "SS", st["q_evac_ss"][1])
        if qq == 0 and h == 0:
            E.w("t", "SS", st["cb_ss"])
            E.w("t", "VV", st["pad_zero_vv"])
        if qq > 0 and h == 0:
            E.w("t", "TT", st["oa_gate_tt"])
        qks = {}
        evs = {}
        qks[0] = emit_qk(h, 0, 0)
        evs[0] = emit_exp(qks[0], 0, 0)
        for kp in range(NP):
            if kp + 1 < NP:
                qks[kp + 1] = emit_qk(h, kp + 1, (kp + 1) % 2)
                evs[kp + 1] = emit_exp(qks[kp + 1], (kp + 1) % 2, (kp + 1) % 3)
            emit_pv(h, kp, kp % 3, evs[kp])
            if kp == 1 and pending is not None:
                epilogue(*pending)
                pending = None
        pending = (h, lps[(NP - 1) % 3])
    epilogue(*pending)
    st["oa_done_vv"] = C["VV"].n
    st["attn_done_tt"] = C["TT"].n


def phA_wo(E, sh, A, st, qq):
    io, C = sh["io"], sh["C"]
    pB, resid = sh["pB"], sh["resid"]

    E.w("g", "TT", st["qtm_gate"][0])
    E.w("g", "TT", st["qtm_gate"][1])
    lw_wo = E.dma("LW", "g", lambda e: e.dma_start(
        out=A["wbuf"][0:64, 0:NH * D], in_=io["wo"][:, :]))
    if qq == 0:
        E.w("t", "VV", st["grp_free_vv"][0])
        E.w("t", "VV", st["grp_free_vv"][1])
        s1 = bcast_vec(E, sh, io["lnv"][2:3, :], A["gbc2"], pB[0])
        st["sb2"] = bcast_vec(E, sh, io["lnv"][3:4, :], A["bbc2"], pB[1])
    E.wt("t", lw_wo)
    E.w("t", "VV", st["oa_done_vv"])
    E.w("t", "SS", st["sb2"])

    evac_vv = [st["grp_free_vv"][0], st["grp_free_vv"][1]]
    for i4 in range(NQ):
        t = qq * NQ + i4
        pw = pB[i4 % 2]
        E.w("t", "VV", evac_vv[i4 % 2])
        tw = 0
        for h in range(NH):
            for half in range(2):
                r = E.t(lambda e, h=h, half=half, i4=i4, pw=pw: e.matmul(
                    pw[:, half * 512:(half + 1) * 512],
                    A["oaT"][:, h * 512 + i4 * 128:h * 512 + (i4 + 1) * 128],
                    A["wbuf"][:, h * D + half * 512:h * D + (half + 1) * 512],
                    start=(h == 0), stop=(h == NH - 1), skip_group_check=True),
                    "TT" if (h == NH - 1 and half == 1) else None)
                tw = r or tw
        E.w("v", "TT", tw)
        if i4 == 0:
            E.w("v", "SS", st["sb2"])
        vln = ln_tile(E, sh, pw[:, 0:D], A["sqs"][:, 0:D], i4 % 2,
                      gain=A["gbc2"][:, 0:D], bias=A["bbc2"][:, 0:D],
                      ln_gate=st["ln_gate"])
        E.w("v", "VV", vln)
        vy = E.v(lambda e, t=t: e.tensor_add(A["sqs"][:, 0:D], A["sqs"][:, 0:D],
                                             resid[:, t * D:(t + 1) * D]), "VV")
        evac_vv[i4 % 2] = vy
        E.w("v", "VV", vy)
        ln_tile(E, sh, A["sqs"][:, 0:D], resid[:, t * D:(t + 1) * D], i4 % 2,
                ln_gate=st["ln_gate"])
    st["wo_done_tt"] = C["TT"].n
    st["oa_gate_tt"] = C["TT"].n
    st["grp_free_vv"] = [C["VV"].n, C["VV"].n]


def phaseB(E, sh, Bd, gate):
    io, C = sh["io"], sh["C"]
    pA, pB, pF = sh["pA"], sh["pB"], sh["pF"]
    resid, ones, ones_bf = sh["resid"], sh["ones"], sh["ones_bf"]
    idf = sh["idf"]

    phase_gate(E, gate)
    lw_gw = E.dma("LW", "g", lambda e: e.dma_start(
        out=Bd["gwT_sb"][:], in_=io["gwT"][:, :]))
    lw_la = E.dma("LW", "g", lambda e: e.dma_start(
        out=Bd["laT_sb"][:], in_=io["laT"][:, :]))
    lw_lb = E.dma("LW", "g", lambda e: e.dma_start(out=Bd["lbT_sb"][:], in_=io["lbT"][:, :]))
    lw_wib = E.dma("LW", "g", lambda e: e.dma_start(out=Bd["wib_sb"][:], in_=io["wib"][:, :]))
    lw_wob = E.dma("LW", "g", lambda e: e.dma_start(out=Bd["wob_sb"][:], in_=io["wob"][:, :]))
    lx_gb = E.dma("LX", "y", lambda e: e.dma_start(out=Bd["gb_sb"][:], in_=io["gbv"][:, :]))

    wo_done_tt = gate["TT"]
    prev_vrs = gate["VV"]
    ln_gate = {}
    wi_lw = {}
    wi_cons = {}
    wo_pb_vv = [gate["VV"], gate["VV"]]
    def emit_wi(E, gid):
        grp = gid % 8
        E.w("g", "TT", wi_cons.get(gid - 2, 0))
        wi_lw[gid] = E.dma("LW", "g", lambda e, grp=grp, gid=gid: e.dma_start(
            out=Bd["wibuf"][:, (gid % 2) * 4096:((gid % 2) + 1) * 4096],
            in_=io["wiT"][grp, :, :]))

    def emit_prep(tg, relu_ss):
        nonlocal prev_vrs
        emit_wi(E, tg * 8 + 0)
        emit_wi(E, tg * 8 + 1)
        rselT_ss = 0
        for tl in range(4):
            t = tg * 4 + tl
            src = resid[:, t * D:(t + 1) * D]
            if t == 0:
                for k2 in ("TT", "VV", "SS"):
                    E.w("t", k2, gate[k2])
            if tl == 0 and relu_ss is not None:
                E.w("t", "SS", relu_ss[0])
                E.w("t", "SS", relu_ss[1])
            ttr = 0
            for c8 in range(8):
                r = E.t(lambda e, c8=c8, src=src: e.transpose(
                    pA[c8 // 4][:, (c8 % 4) * 128:(c8 % 4 + 1) * 128],
                    src[:, c8 * 128:(c8 + 1) * 128], idf[:]),
                    "TT" if c8 == 7 else None)
                ttr = r or ttr
            E.w("s", "TT", ttr)
            sx = 0
            for c8 in range(8):
                r = E.s(lambda e, c8=c8, tl=tl: e.copy(
                    out=Bd["x1T"][:, c8 * 512 + tl * 128:c8 * 512 + (tl + 1) * 128],
                    in_=pA[c8 // 4][:, (c8 % 4) * 128:(c8 % 4 + 1) * 128]),
                    "SS" if c8 == 7 else None)
                sx = r or sx

            E.w("t", "SS", sx)
            if t == 0:
                E.wt("t", lw_gw)
                E.wt("t", lw_la)
                E.wt("t", lx_gb)
            for c8 in range(8):
                E.t(lambda e, c8=c8, tl=tl: e.matmul(
                    pA[2][:, 0:NE],
                    Bd["x1T"][:, c8 * 512 + tl * 128:c8 * 512 + (tl + 1) * 128],
                    Bd["gwT_sb"][:, c8 * NE:(c8 + 1) * NE],
                    start=(c8 == 0), stop=False, skip_group_check=True))
            tlg = E.t(lambda e: e.matmul(
                pA[2][:, 0:NE], ones[0:1, 0:128], Bd["gb_sb"][0:1, 0:NE],
                start=False, stop=True, skip_group_check=True), "TT")
            E.w("t", "VV", prev_vrs)
            tra = 0
            for c8 in range(8):
                r = E.t(lambda e, c8=c8, tl=tl: e.matmul(
                    pA[1][:, 0:32],
                    Bd["x1T"][:, c8 * 512 + tl * 128:c8 * 512 + (tl + 1) * 128],
                    Bd["laT_sb"][:, c8 * 32:(c8 + 1) * 32],
                    start=(c8 == 0), stop=(c8 == 7), skip_group_check=True),
                    "TT" if c8 == 7 else None)
                tra = r or tra

            E.w("v", "TT", tlg)
            rank_counts(E, sh, Bd, pA[2][:, 0:NE])
            vmk = E.v(lambda e: e.tensor_scalar(out=Bd["msk"][:], in0=Bd["cA"][:],
                                                scalar1=0.5, scalar2=None,
                                                op0=OP.is_lt), "VV")
            E.w("v", "VV", vmk)
            E.w("v", "TT", tra)
            vrs = 0
            for ex in range(NE):
                r = E.v(lambda e, ex=ex: e.tensor_scalar(
                    out=Bd["rsel"][:, ex * 4:(ex + 1) * 4],
                    in0=pA[1][:, ex * 4:(ex + 1) * 4],
                    scalar1=Bd["msk"][:, ex:ex + 1], scalar2=None, op0=OP.mult),
                    "VV" if ex == NE - 1 else None)
                vrs = r or vrs
            prev_vrs = vrs
            E.w("t", "VV", vrs)
            trs = E.t(lambda e: e.transpose(pA[2][0:32, 128:256], Bd["rsel"][:, 0:32],
                                            idf[:]), "TT")
            E.w("s", "TT", trs)
            rselT_ss = E.s(lambda e, tl=tl: e.copy(
                out=Bd["rselT"][0:32, tl * 128:(tl + 1) * 128],
                in_=pA[2][0:32, 128:256]), "SS")
        return rselT_ss

    rsel_map = {0: emit_prep(0, None)}
    for tg in range(4):
        rselT_ss = rsel_map[tg]
        relu_ss = [0, 0]
        for p in range(32):
            gid = tg * 8 + p // 4
            pg = pA[p % 2]
            E.wt("t", wi_lw[gid])
            E.w("t", "SS", relu_ss[p % 2])
            if p == 0:
                E.wt("t", lw_wib)
                E.wt("t", lw_lb)
                E.w("t", "SS", rselT_ss)
            for c8 in range(8):
                E.t(lambda e, c8=c8, p=p, gid=gid, pg=pg: e.matmul(
                    pg[:, 0:512],
                    Bd["wibuf"][:, (gid % 2) * 4096 + c8 * 512 + (p % 4) * 128:
                                (gid % 2) * 4096 + c8 * 512 + (p % 4 + 1) * 128],
                    Bd["x1T"][:, c8 * 512:(c8 + 1) * 512],
                    start=(c8 == 0), stop=False, skip_group_check=True))
            tbp = E.t(lambda e, p=p, pg=pg: e.matmul(
                pg[:, 0:512], Bd["lbT_sb"][0:32, p * 128:(p + 1) * 128],
                Bd["rselT"][0:32, 0:512],
                start=False, stop=True, skip_group_check=True), "TT")
            if p % 4 == 3:
                wi_cons[gid] = tbp
                if p // 4 + 2 <= 7:
                    emit_wi(E, tg * 8 + p // 4 + 2)
            E.w("s", "TT", tbp)
            relu_ss[p % 2] = E.s(lambda e, p=p, pg=pg: e.activation(
                out=Bd["interT"][:, p * 512:(p + 1) * 512], in_=pg[:, 0:512],
                func=AF.Relu, bias=Bd["wib_sb"][:, p:p + 1]), "SS")

        wo_lw = {}
        wo_cons = {}

        def emit_wo(E, wid):
            pg8 = wid % 8
            E.w("g", "TT", wo_cons.get(wid - 2, 0))
            wo_lw[wid] = E.dma("LW", "g", lambda e, pg8=pg8, wid=wid: e.dma_start(
                out=Bd["wobuf"][:, (wid % 2) * 4096:((wid % 2) + 1) * 4096],
                in_=io["woT"][pg8, :, :]))

        for tlp in range(2):
            emit_wo(E, tlp * 8 + 0)
            emit_wo(E, tlp * 8 + 1)
            for pg8 in range(8):
                wid = tlp * 8 + pg8
                E.wt("t", wo_lw[wid])
                if pg8 == 0:
                    E.w("t", "SS", relu_ss[0])
                    E.w("t", "SS", relu_ss[1])
                    E.wt("t", lw_wob)
                    E.w("t", "VV", wo_pb_vv[0])
                    E.w("t", "VV", wo_pb_vv[1])
                wcv = 0
                for pp in range(4):
                    p = pg8 * 4 + pp
                    for tli in range(2):
                        tl = tlp * 2 + tli
                        for half in range(2):
                            r = E.t(lambda e, p=p, pp=pp, tl=tl, tli=tli, half=half, wid=wid: e.matmul(
                                pB[tli][:, half * 512:(half + 1) * 512],
                                Bd["interT"][:, p * 512 + tl * 128:p * 512 + (tl + 1) * 128],
                                Bd["wobuf"][:, (wid % 2) * 4096 + pp * 1024 + half * 512:
                                            (wid % 2) * 4096 + pp * 1024 + (half + 1) * 512],
                                start=(p == 0), stop=False, skip_group_check=True),
                                "TT" if (pp == 3 and tli == 1 and half == 1) else None)
                            wcv = r or wcv
                wo_cons[wid] = wcv
                if pg8 + 2 <= 7:
                    emit_wo(E, tlp * 8 + pg8 + 2)
            twn = 0
            for tli in range(2):
                for half in range(2):
                    r = E.t(lambda e, tli=tli, half=half: e.matmul(
                        pB[tli][:, half * 512:(half + 1) * 512], ones_bf[0:1, 0:128],
                        Bd["wob_sb"][0:1, half * 512:(half + 1) * 512],
                        start=False, stop=(half == 1), skip_group_check=True),
                        "TT" if (tli == 1 and half == 1) else None)
                    twn = r or twn
            for tli in range(2):
                t = tg * 4 + tlp * 2 + tli
                E.w("v", "TT", twn)
                vy = E.v(lambda e, t=t, tli=tli: e.tensor_add(
                    Bd["ytmp"][:, tli * D:(tli + 1) * D], pB[tli][:, 0:D],
                    resid[:, t * D:(t + 1) * D]), "VV")
                E.w("v", "VV", vy)
                vf = ln_tile(E, sh, Bd["ytmp"][:, tli * D:(tli + 1) * D],
                             resid[:, t * D:(t + 1) * D], tli, ln_gate=ln_gate)
                wo_pb_vv[tli] = vy
            if tlp == 0 and tg < 3:
                rsel_map[tg + 1] = emit_prep(tg + 1, relu_ss)
        wo_done_tt = C["TT"].n


def phaseC(E, sh, Cd, gate):
    io, C = sh["io"], sh["C"]
    pA, pB, pF = sh["pA"], sh["pB"], sh["pF"]
    resid, ones, idf, idb = sh["resid"], sh["ones"], sh["idf"], sh["idb"]
    gbc, bbc, rflag = Cd["gbc"], Cd["bbc"], sh["rflag"]

    phase_gate(E, gate)
    lww = [0] * 5
    for i in range(2):
        lww[i] = E.dma("LW", "g", lambda e, i=i: e.dma_start(
            out=Cd["w_sb"][i % 2][:], in_=io["w1"][i, :, :]))

    for k2 in ("TT", "VV", "SS"):
        E.w("t", k2, gate[k2])
    sb2a = bcast_vec(E, sh, io["lnv"][4:5, :], gbc, pB[0])
    sb2 = bcast_vec(E, sh, io["lnv"][5:6, :], bbc, pB[1])

    src15 = resid[:, 15 * D:16 * D]
    ttr = 0
    for c8 in range(8):
        r = E.t(lambda e, c8=c8: e.transpose(
            pA[c8 // 4][:, (c8 % 4) * 128:(c8 % 4 + 1) * 128],
            src15[:, c8 * 128:(c8 + 1) * 128], idf[:]), "TT" if c8 == 7 else None)
        ttr = r or ttr
    E.w("s", "TT", ttr)
    sx15 = 0
    for c8 in range(8):
        r = E.s(lambda e, c8=c8: e.copy(
            out=Cd["xT15"][:, c8 * 128:(c8 + 1) * 128],
            in_=pA[c8 // 4][:, (c8 % 4) * 128:(c8 % 4 + 1) * 128]),
            "SS" if c8 == 7 else None)
        sx15 = r or sx15

    prj_ss = []
    for i in range(4):
        E.w("t", "SS", sx15 if i == 0 else 0)
        E.wt("t", lww[i])
        pw = pB[i % 2]
        if i == 0:
            E.w("t", "SS", sb2)
        if i >= 2:
            E.w("t", "SS", prj_ss[i - 2])
        tp = 0
        for half in range(2):
            for c8 in range(8):
                r = E.t(lambda e, c8=c8, half=half, i=i, pw=pw: e.matmul(
                    pw[:, half * 512:(half + 1) * 512],
                    Cd["xT15"][:, c8 * 128:(c8 + 1) * 128],
                    Cd["w_sb"][i % 2][:, c8 * D + half * 512:c8 * D + (half + 1) * 512],
                    start=(c8 == 0), stop=(c8 == 7), skip_group_check=True),
                    "TT" if (half == 1 and c8 == 7) else None)
                tp = r or tp
        if i + 2 <= 4:
            E.w("g", "TT", tp)
            lww[i + 2] = E.dma("LW", "g", lambda e, i=i: e.dma_start(
                out=Cd["w_sb"][i % 2][:], in_=io["w1"][i + 2, :, :]))
        E.w("s", "TT", tp)
        prj_ss.append(E.s(lambda e, i=i, pw=pw: e.copy(out=Cd["tm"][i][:, 0:D],
                                                       in_=pw[:, 0:D]), "SS"))

    for i in range(3):
        E.w("v", "SS", prj_ss[i])
        vs = E.v(lambda e, i=i: e.tensor_mul(Cd["Af"][:, 0:D], Cd["tm"][i][:, 0:D],
                                             Cd["tm"][i][:, 0:D]), "VV")
        E.w("v", "VV", vs)
        vr = E.v(lambda e, i=i: e.tensor_reduce(
            Cd["ssq"][:, i * NH:(i + 1) * NH],
            Cd["Af"][:, 0:D].rearrange("p (h dd) -> p h dd", h=NH),
            axis=AX, op=OP.add), "VV")
        E.w("s", "VV", vr)
        s2a = E.s(lambda e, i=i: e.activation(out=Cd["ssq"][:, 3 * NH:4 * NH],
                                              in_=Cd["ssq"][:, i * NH:(i + 1) * NH], func=AF.Ln), "SS")
        E.w("s", "SS", s2a)
        s2 = E.s(lambda e, i=i: e.activation(out=Cd["ssq"][:, i * NH:(i + 1) * NH],
                                             in_=Cd["ssq"][:, 3 * NH:4 * NH],
                                             func=AF.Exp, scale=0.5), "SS")
        E.w("v", "SS", s2)
        va = E.v(lambda e, i=i: e.tensor_scalar_add(
            Cd["ssq"][:, i * NH:(i + 1) * NH], Cd["ssq"][:, i * NH:(i + 1) * NH], 1e-7), "VV")
        E.w("v", "VV", va)
        vb = E.v(lambda e, i=i: e.reciprocal(Cd["ssq"][:, i * NH:(i + 1) * NH],
                                             Cd["ssq"][:, i * NH:(i + 1) * NH]), "VV")
        E.w("v", "VV", vb)
        for h in range(NH):
            E.v(lambda e, i=i, h=h: e.tensor_scalar(
                out=Cd["bm"][i][:, h * DH:(h + 1) * DH],
                in0=Cd["tm"][i][:, h * DH:(h + 1) * DH],
                scalar1=Cd["ssq"][:, i * NH + h:i * NH + h + 1],
                scalar2=None, op0=OP.mult), "VV" if h == NH - 1 else None)
    E.w("v", "SS", prj_ss[3])
    bm_vv = E.v(lambda e: e.tensor_copy(out=Cd["bm"][3][:, 0:D], in_=Cd["tm"][3][:, 0:D]), "VV")

    tz_prev = 0
    vmn_prev = bm_vv
    ga_prev = 0
    vm_prev = 0
    zev_ss = 0
    for h in range(NH):
        E.w("t", "VV", vmn_prev)
        E.w("t", "SS", zev_ss)
        tf = 0
        for i in range(4):
            r = E.t(lambda e, i=i, h=h: e.transpose(
                pF[0:64, i * 128:(i + 1) * 128], Cd["bm"][i][:, h * DH:(h + 1) * DH], idb[:]),
                "TT" if i == 3 else None)
            tf = r or tf
        E.w("s", "TT", tf)
        for i in range(4):
            E.s(lambda e, i=i: e.copy(out=Cd["fm"][i][0:64, 0:128],
                                      in_=pF[0:64, i * 128:(i + 1) * 128]))
        E.s(lambda e: e.copy(out=Cd["fmf"][0][0:64, 0:128], in_=pF[0:64, 1 * 128:2 * 128]))
        sf = E.s(lambda e: e.copy(out=Cd["fmf"][1][0:64, 0:128],
                                  in_=pF[0:64, 3 * 128:4 * 128]), "SS")
        E.w("v", "SS", sf)
        for j in range(32):
            E.v(lambda e, j=j: e.tensor_scalar(
                out=Cd["Mh"][0:64, j * 32:(j + 1) * 32],
                in0=Cd["fm"][2][0:64, 96:128],
                scalar1=Cd["fmf"][0][0:64, 96 + j:97 + j], scalar2=None, op0=OP.mult))
        vmn = 0
        for j in range(32):
            r = E.v(lambda e, j=j: e.tensor_scalar(
                out=Cd["Nh"][0:64, j * 32:(j + 1) * 32],
                in0=Cd["fm"][3][0:64, 96:128],
                scalar1=Cd["fmf"][1][0:64, 96 + j:97 + j], scalar2=None, op0=OP.mult),
                "VV" if j == 31 else None)
            vmn = r or vmn
        vmn_prev = vmn
        E.w("t", "VV", vmn)
        ta = 0
        for half in range(2):
            r = E.t(lambda e, half=half: e.matmul(
                pB[0][:, half * 512:(half + 1) * 512], Cd["fm"][0][0:64, 0:128],
                Cd["Mh"][0:64, half * 512:(half + 1) * 512],
                start=True, stop=True, skip_group_check=True),
                "TT" if half == 1 else None)
            ta = r or ta
        E.w("s", "TT", ta)
        E.w("s", "GG", ga_prev)
        E.w("s", "VV", vm_prev)
        sa = E.s(lambda e: e.copy(out=Cd["Af"][:, 0:1024], in_=pB[0][:, 0:1024]), "SS")
        E.w("g", "SS", sa)
        ga0 = E.g(lambda e: e.affine_select(out=Cd["Af"][:], in_=Cd["Af"][:],
                                            pattern=[[-1, 32], [0, 32]], base=96,
                                            channel_multiplier=1,
                                            compare_op=OP.is_ge, fill=-1e30), "GG")
        E.w("g", "GG", ga0)
        ga = E.g(lambda e: e.affine_select(out=Cd["Af"][:], in_=Cd["Af"][:],
                                           pattern=[[0, 32], [-1, 32]], base=96,
                                           channel_multiplier=1,
                                           compare_op=OP.is_ge, fill=-1e30), "GG")
        ga_prev = ga
        E.w("v", "GG", ga)
        vm = E.v(lambda e: e.reduce_max(Cd["smx"][:, 0:1], Cd["Af"][:], axis=AX,
                                        negate=True), "VV")
        E.w("v", "VV", vm)
        vm8 = E.v(lambda e: e.tensor_scalar_mul(Cd["smx"][:, 1:2], Cd["smx"][:, 0:1],
                                                0.125), "VV")
        vm_prev = vm8
        E.w("s", "VV", vm8)
        sp = E.s(lambda e: e.activation(out=Cd["Pf"][:, 0:1024], in_=Cd["Af"][:],
                                        func=AF.Exp, bias=Cd["smx"][:, 1:2], scale=0.125,
                                        accum_out=Cd["smx"][:, 2:3]), "SS")
        E.w("v", "SS", sp)
        vrp = E.v(lambda e: e.reciprocal(Cd["smx"][:, 3:4], Cd["smx"][:, 2:3]), "VV")
        E.w("v", "VV", vrp)
        vpb = E.v(lambda e: e.tensor_scalar(out=Cd["Pb"][:], in0=Cd["Pf"][:, 0:1024],
                                            scalar1=Cd["smx"][:, 3:4], scalar2=None,
                                            op0=OP.mult), "VV")
        E.w("t", "VV", vpb)
        E.w("t", "TT", tz_prev)
        tpb = 0
        for c8 in range(8):
            r = E.t(lambda e, c8=c8: e.transpose(pF[:, c8 * 128:(c8 + 1) * 128],
                                                 Cd["Pb"][:, c8 * 128:(c8 + 1) * 128], idb[:]),
                    "TT" if c8 == 7 else None)
            tpb = r or tpb
        E.w("s", "TT", tpb)
        spb = E.s(lambda e: e.copy(out=Cd["pT"][:, 0:1024], in_=pF[:, 0:1024]), "SS")
        E.w("t", "SS", spb)
        tnm = 0
        for c8 in range(8):
            r = E.t(lambda e, c8=c8: e.transpose(pF[:, c8 * 64:(c8 + 1) * 64],
                                                 Cd["Nh"][0:64, c8 * 128:(c8 + 1) * 128], idb[0:64, 0:64]),
                    "TT" if c8 == 7 else None)
            tnm = r or tnm
        E.w("s", "TT", tnm)
        snm = E.s(lambda e: e.copy(out=Cd["Ntm"][:, 0:512], in_=pF[:, 0:512]), "SS")
        E.w("t", "SS", snm)
        tz = 0
        for c8 in range(8):
            r = E.t(lambda e, c8=c8: e.matmul(
                pA[0][:, 0:64], Cd["pT"][:, c8 * 128:(c8 + 1) * 128],
                Cd["Ntm"][:, c8 * 64:(c8 + 1) * 64],
                start=(c8 == 0), stop=(c8 == 7), skip_group_check=True),
                "TT" if c8 == 7 else None)
            tz = r or tz
        tz_prev = tz
        E.w("s", "TT", tz)
        zev_ss = E.s(lambda e, h=h: e.copy(out=Cd["Ztm"][:, h * DH:(h + 1) * DH],
                                           in_=pA[0][:, 0:64]), "SS")

    E.w("t", "SS", zev_ss)
    E.wt("t", lww[4])
    tzt = 0
    for c8 in range(8):
        r = E.t(lambda e, c8=c8: e.transpose(pF[:, c8 * 128:(c8 + 1) * 128],
                                             Cd["Ztm"][:, c8 * 128:(c8 + 1) * 128], idb[:]),
                "TT" if c8 == 7 else None)
        tzt = r or tzt
    E.w("s", "TT", tzt)
    szt = E.s(lambda e: e.copy(out=Cd["ZT"][:, 0:1024], in_=pF[:, 0:1024]), "SS")
    E.w("t", "SS", szt)
    tzp = 0
    for half in range(2):
        for c8 in range(8):
            r = E.t(lambda e, c8=c8, half=half: e.matmul(
                pB[0][:, half * 512:(half + 1) * 512],
                Cd["ZT"][:, c8 * 128:(c8 + 1) * 128],
                Cd["w_sb"][0][:, c8 * D + half * 512:c8 * D + (half + 1) * 512],
                start=(c8 == 0), stop=(c8 == 7), skip_group_check=True),
                "TT" if (half == 1 and c8 == 7) else None)
            tzp = r or tzp
    E.w("s", "TT", tzp)
    szp = E.s(lambda e: e.mul(out=Cd["zmask"][:, 0:D], in_=pB[0][:, 0:D],
                              mul=rflag[:, 0:1]), "SS")
    E.w("v", "SS", szp)
    vti = E.v(lambda e: e.tensor_add(Cd["tin"][:, 0:D], Cd["zmask"][:, 0:D],
                                     resid[:, 15 * D:16 * D]), "VV")

    ln_gate = {}
    for t in range(NT):
        src = Cd["tin"][:, 0:D] if t == 15 else resid[:, t * D:(t + 1) * D]
        pre = vti if t == 15 else 0
        if t == 0:
            E.w("v", "SS", sb2)
        va = ln_tile(E, sh, src, Cd["att"][:, 0:D], t % 2,
                     gain=gbc[:, 0:D], bias=bbc[:, 0:D], ln_gate=ln_gate,
                     pre_v_wait=pre)
        E.w("v", "VV", va)
        vy = E.v(lambda e, t=t: e.tensor_add(Cd["Pf"][:, 0:D], Cd["att"][:, 0:D],
                                             resid[:, t * D:(t + 1) * D]), "VV")
        E.w("v", "VV", vy)
        ln_tile(E, sh, Cd["Pf"][:, 0:D], resid[:, t * D:(t + 1) * D], t % 2,
                ln_gate=ln_gate)


def phaseD(E, sh, Dd, gate):
    io, C = sh["io"], sh["C"]
    pA, pB, pF = sh["pA"], sh["pB"], sh["pF"]
    resid, ones, ones_bf, idf = sh["resid"], sh["ones"], sh["ones_bf"], sh["idf"]

    phase_gate(E, gate)
    lw_rw = E.dma("LW", "g", lambda e: e.dma_start(
        out=Dd["rwT_sb"][:], in_=io["rwT"][:, :]))
    lx_rb = E.dma("LX", "y", lambda e: e.dma_start(out=Dd["rb_sb"][:], in_=io["rbv"][:, :]))

    rt_ss = 0
    sx_prev = 0
    for t in range(NT):
        if t == 0:
            for k2 in ("TT", "VV", "SS"):
                E.w("t", k2, gate[k2])
        E.w("t", "SS", sx_prev)
        ttr = 0
        for c8 in range(8):
            r = E.t(lambda e, c8=c8, t=t: e.transpose(
                pA[c8 // 4][:, (c8 % 4) * 128:(c8 % 4 + 1) * 128],
                resid[:, t * D + c8 * 128:t * D + (c8 + 1) * 128], idf[:]),
                "TT" if c8 == 7 else None)
            ttr = r or ttr
        E.w("s", "TT", ttr)
        sx = 0
        for c8 in range(8):
            r = E.s(lambda e, c8=c8, t=t: e.copy(
                out=Dd["x3T"][:, c8 * HALF + t * 128:c8 * HALF + (t + 1) * 128],
                in_=pA[c8 // 4][:, (c8 % 4) * 128:(c8 % 4 + 1) * 128]),
                "SS" if c8 == 7 else None)
            sx = r or sx
        sx_prev = sx
        E.w("t", "SS", sx)
        E.w("t", "SS", rt_ss)
        if t == 0:
            E.wt("t", lw_rw)
            E.wt("t", lx_rb)
        for c8 in range(8):
            E.t(lambda e, c8=c8, t=t: e.matmul(
                pA[2][:, 0:NE],
                Dd["x3T"][:, c8 * HALF + t * 128:c8 * HALF + (t + 1) * 128],
                Dd["rwT_sb"][:, c8 * NE:(c8 + 1) * NE],
                start=(c8 == 0), stop=False, skip_group_check=True))
        tlg = E.t(lambda e: e.matmul(pA[2][:, 0:NE], ones[0:1, 0:128],
                                     Dd["rb_sb"][0:1, 0:NE],
                                     start=False, stop=True, skip_group_check=True), "TT")
        E.w("v", "TT", tlg)
        rank_counts(E, sh, Dd, pA[2][:, 0:NE])
        E.v(lambda e: e.tensor_scalar(out=Dd["sel"][:], in0=Dd["cA"][:],
                                      scalar1=3.5, scalar2=None, op0=OP.is_lt))
        vnm = E.v(lambda e: e.reduce_max(Dd["nmx"][:, 0:1], Dd["lgt"][:], axis=AX,
                                         negate=True), "VV")
        E.w("s", "VV", vnm)
        rt_ss = E.s(lambda e: e.activation(out=Dd["pex"][:], in_=Dd["lgt"][:],
                                           func=AF.Exp, bias=Dd["nmx"][:, 0:1],
                                           scale=1.0), "SS")
        E.w("v", "SS", rt_ss)
        vpm = E.v(lambda e: e.tensor_mul(Dd["pex"][:], Dd["pex"][:], Dd["sel"][:]), "VV")
        E.w("v", "VV", vpm)
        vdn = E.v(lambda e: e.reduce_sum(Dd["nmx"][:, 1:2], Dd["pex"][:], axis=AX), "VV")
        E.w("v", "VV", vdn)
        vrc = E.v(lambda e: e.reciprocal(Dd["nmx"][:, 1:2], Dd["nmx"][:, 1:2]), "VV")
        E.w("v", "VV", vrc)
        E.v(lambda e, t=t: e.tensor_scalar(out=Dd["ncomb"][:, t * NE:(t + 1) * NE],
                                           in0=Dd["pex"][:],
                                           scalar1=Dd["nmx"][:, 1:2], scalar2=-1.0,
                                           op0=OP.mult, op1=OP.mult), "VV")
    comb_vv = C["VV"].n

    st_vv = comb_vv
    pair_ss = [0, 0]
    pair_vv = [0, 0]
    act_vv = [0, 0]
    dn_vv = [0, 0]
    dn_cons_tt = 0
    gub_cons_tt = 0
    dnb_cons = 0
    gub_free_ss = 0
    gub_free_vv = 0
    gup1_free = 0
    gu_lw = {}
    gu_cons = {}

    def emit_gub(ei2, gcons, gfss, gfvv, gpf):
        E.w("g", "TT", gcons)
        E.w("g", "SS", gfss)
        E.w("g", "VV", gfvv)
        lw = E.dma("LW", "g", lambda e, ei2=ei2: e.dma_start(
            out=Dd["gub_sb"][:], in_=io["gub"][ei2, :, :]))
        E.wt("v", lw)
        E.w("v", "VV", gpf)
        E.v(lambda e: e.tensor_scalar(out=Dd["g7mb"][:], in0=Dd["gub_sb"][:, 0:16],
                                      scalar1=-1.0, scalar2=7.0,
                                      op0=OP.mult, op1=OP.add))
        return E.v(lambda e: e.tensor_scalar_add(Dd["gup1"][:],
                                                 Dd["gub_sb"][:, 16:32], 1.0), "VV")

    gub_vv_next = emit_gub(0, 0, 0, 0, 0)
    for ei in range(NE):
        gub_vv = gub_vv_next
        for j0 in range(2):
            gid0 = (ei * 2) * 16 + j0
            E.w("g", "TT", gu_cons.get(gid0 - 2, 0))
            gu_lw[gid0] = E.dma("LW", "g", lambda e, ei=ei, j0=j0: e.dma_start(
                out=Dd["wgu"][gid0 % 2][:], in_=io["guT"][ei, j0, :, :]))
        E.w("g", "TT", dn_cons_tt)
        dn_lw = E.dma("LW", "g", lambda e, ei=ei: e.dma_start(
            out=Dd["dnT_sb"][:], in_=io["dnT"][ei, :, :]))
        E.w("y", "VV", dnb_cons)
        db = E.dma("LX", "y", lambda e, ei=ei: e.dma_start(
            out=Dd["dnb_st"][:], in_=io["dnb"][0:1, ei * D:(ei + 1) * D]))
        E.wt("v", db)
        E.w("v", "TT", dn_cons_tt)
        vneg = E.v(lambda e: e.tensor_scalar_mul(Dd["dnbn"][0:1, 0:D],
                                                 Dd["dnb_st"][0:1, 0:D], -1.0), "VV")
        dnb_cons = vneg

        for tgh in range(2):
            it = 0
            for j in range(16):
                gid = (ei * 2 + tgh) * 16 + j
                if tgh > 0 or j >= 2:
                    E.w("g", "TT", gu_cons.get(gid - 2, 0))
                    gu_lw[gid] = E.dma("LW", "g", lambda e, ei=ei, j=j, gid=gid: e.dma_start(
                        out=Dd["wgu"][gid % 2][:], in_=io["guT"][ei, j, :, :]))
                for blk in range(2):
                        par = it % 2
                        pg_ap = pA[0][:, 0:512] if par == 0 else pA[2][:, 0:512]
                        pu_ap = pA[1][:, 0:512] if par == 0 else pB[0][:, 0:512]
                        E.wt("t", gu_lw[gid])
                        E.w("t", "SS", pair_ss[par])
                        E.w("t", "VV", pair_vv[par])
                        if it == 0:
                            E.w("t", "VV", gub_vv)
                        if it == 1:
                            E.w("t", "VV", st_vv)
                        xoff = tgh * 1024 + blk * 512
                        for c8 in range(8):
                            E.t(lambda e, c8=c8, gid=gid, pg_ap=pg_ap, xoff=xoff: e.matmul(
                                pg_ap,
                                Dd["wgu"][gid % 2][:, c8 * 128:(c8 + 1) * 128],
                                Dd["x3T"][:, c8 * HALF + xoff:c8 * HALF + xoff + 512],
                                start=(c8 == 0), stop=(c8 == 7), skip_group_check=True))
                        tgu = 0
                        for c8 in range(8):
                            r = E.t(lambda e, c8=c8, gid=gid, pu_ap=pu_ap, xoff=xoff: e.matmul(
                                pu_ap,
                                Dd["wgu"][gid % 2][:, 1024 + c8 * 128:1024 + (c8 + 1) * 128],
                                Dd["x3T"][:, c8 * HALF + xoff:c8 * HALF + xoff + 512],
                                start=(c8 == 0), stop=(c8 == 7), skip_group_check=True),
                                "TT" if c8 == 7 else None)
                            tgu = r or tgu
                        if blk == 1:
                            gu_cons[gid] = tgu
                        so = par * 512
                        E.w("s", "TT", tgu)
                        E.w("s", "VV", act_vv[par])
                        sr7 = E.s(lambda e, so=so, pg_ap=pg_ap, j=j: e.activation(
                            out=Dd["r7"][:, so:so + 512], in_=pg_ap,
                            func=AF.Relu, scale=-1.0,
                            bias=Dd["g7mb"][:, j:j + 1]), "SS")
                        E.w("s", "SS", sr7)
                        sgs = E.s(lambda e, so=so: e.activation(
                            out=Dd["sg"][:, so:so + 512], in_=Dd["r7"][:, so:so + 512],
                            func=AF.Sigmoid, scale=-1.702, bias=1.702 * 7.0), "SS")
                        pair_ss[par] = sgs
                        E.w("v", "SS", sgs)
                        vt1 = E.v(lambda e, so=so: e.scalar_tensor_tensor(
                            out=Dd["t1"][:, so:so + 512], in0=Dd["r7"][:, so:so + 512],
                            scalar=7.0, in1=Dd["sg"][:, so:so + 512],
                            op0=OP.subtract, op1=OP.mult), "VV")
                        E.w("v", "VV", vt1)
                        vuc = E.v(lambda e, so=so, pu_ap=pu_ap, j=j: e.tensor_scalar(
                            out=Dd["r7"][:, so:so + 512], in0=pu_ap,
                            scalar1=Dd["gup1"][:, j:j + 1], scalar2=8.0,
                            op0=OP.add, op1=OP.min), "VV")
                        pair_vv[par] = vuc
                        E.w("v", "VV", vuc)
                        aoff = j * 1024 + blk * 512
                        act_vv[par] = E.v(lambda e, so=so, aoff=aoff: e.scalar_tensor_tensor(
                            out=Dd["actT"][:, aoff:aoff + 512], in0=Dd["r7"][:, so:so + 512],
                            scalar=-6.0, in1=Dd["t1"][:, so:so + 512],
                            op0=OP.max, op1=OP.mult), "VV")
                        it += 1
            act_done = max(act_vv)
            gub_cons_tt = C["TT"].n
            if tgh == 1 and ei + 1 < NE:
                gub_vv_next = emit_gub(ei + 1, gub_cons_tt, max(pair_ss),
                                       max(pair_vv), max(pair_vv))
            E.w("t", "VV", act_done)
            E.wt("t", dn_lw)
            E.w("t", "VV", vneg)
            for t8 in range(8):
                T = tgh * 8 + t8
                pw = pB[t8 % 2]
                E.w("t", "VV", dn_vv[t8 % 2])
                for kc in range(16):
                    for half in range(2):
                        E.t(lambda e, kc=kc, half=half, t8=t8, pw=pw: e.matmul(
                            pw[:, half * 512:(half + 1) * 512],
                            Dd["actT"][:, kc * 1024 + (t8 // 4) * 512 + (t8 % 4) * 128:
                                       kc * 1024 + (t8 // 4) * 512 + (t8 % 4 + 1) * 128],
                            Dd["dnT_sb"][:, kc * D + half * 512:kc * D + (half + 1) * 512],
                            start=(kc == 0), stop=False, skip_group_check=True))
                tdn = 0
                for half in range(2):
                    r = E.t(lambda e, half=half, pw=pw: e.matmul(
                        pw[:, half * 512:(half + 1) * 512], ones_bf[0:1, 0:128],
                        Dd["dnbn"][0:1, half * 512:(half + 1) * 512],
                        start=False, stop=(half == 1), skip_group_check=True),
                        "TT" if half == 1 else None)
                    tdn = r or tdn
                E.w("v", "TT", tdn)
                dn_vv[t8 % 2] = E.v(lambda e, T=T, ei=ei, pw=pw: e.scalar_tensor_tensor(
                    out=resid[:, T * D:(T + 1) * D], in0=pw[:, 0:D],
                    scalar=Dd["ncomb"][:, T * NE + ei:T * NE + ei + 1],
                    in1=resid[:, T * D:(T + 1) * D], op0=OP.mult, op1=OP.add), "VV")
                st_vv = dn_vv[t8 % 2]
            dn_cons_tt = C["TT"].n
        gub_free_ss = max(pair_ss)
        gup1_free = max(pair_vv)
        gub_free_vv = max(pair_vv)


def phaseOut(E, sh, outb, gate):
    C = sh["C"]
    resid, io = sh["resid"], sh["io"]
    ln_gate = {}
    sto = [None, None]
    E.w("v", "VV", gate["VV"])
    for t in range(NT):
        sl = t % 2
        E.wt("v", sto[sl])
        if PHASES >= 4:
            vf = ln_tile(E, sh, resid[:, t * D:(t + 1) * D], outb[:, sl * D:(sl + 1) * D],
                         sl, ln_gate=ln_gate)
        else:
            vf = E.v(lambda e, t=t, sl=sl: e.tensor_copy(
                out=outb[:, sl * D:(sl + 1) * D],
                in_=resid[:, t * D:(t + 1) * D]), "VV")
        E.w("y", "VV", vf)
        sto[sl] = E.dma("STO", "y", lambda e, t=t, sl=sl: e.dma_start(
            out=io["out"][t * 128:(t + 1) * 128, :], in_=outb[:, sl * D:(sl + 1) * D]))


_NC_CACHE = {}


def host_prep(inputs):
    def f32(x):
        return np.ascontiguousarray(np.asarray(x), dtype=np.float32)

    lnv = np.zeros((6, D), np.float32)
    lnv[0] = f32(inputs["L0_attn_ln1_g"]); lnv[1] = f32(inputs["L0_attn_ln1_b"])
    lnv[2] = f32(inputs["L0_attn_ln2_g"]); lnv[3] = f32(inputs["L0_attn_ln2_b"])
    lnv[4] = f32(inputs["L1_attn_ln_gamma"]); lnv[5] = f32(inputs["L1_attn_ln_beta"])
    def ktile(w):
        K, N = w.shape
        return np.ascontiguousarray(
            w.reshape(K // 128, 128, N).transpose(1, 0, 2).reshape(128, -1))

    wiT0 = f32(inputs["L0_ffn_wi_W"]).T
    wiTt = np.ascontiguousarray(
        wiT0.reshape(8, 128, 8, 512).transpose(2, 1, 0, 3).reshape(8, 128, 4096))
    woT0 = f32(inputs["L0_ffn_wo_W"]).T
    woTt = np.ascontiguousarray(
        woT0.reshape(8, 4, 128, D).transpose(0, 2, 1, 3).reshape(8, 128, 4 * D))
    gu0 = f32(inputs["L1_ffn_gate_up_proj"]).transpose(0, 2, 1)
    gu5 = gu0.reshape(NE, 8, 128, 32, 128)
    gt = gu5[:, :, :, :16].transpose(0, 3, 2, 1, 4).reshape(NE, 16, 128, 1024)
    ut = gu5[:, :, :, 16:].transpose(0, 3, 2, 1, 4).reshape(NE, 16, 128, 1024)
    guTt = np.ascontiguousarray(np.concatenate([gt, ut], axis=3))
    dn0 = f32(inputs["L1_ffn_down_proj"]).transpose(0, 2, 1)
    dnTt = np.ascontiguousarray(
        dn0.reshape(NE, 16, 128, D).transpose(0, 2, 1, 3).reshape(NE, 128, 16 * D))
    wo0 = f32(inputs["L0_attn_Wo"])
    wot = np.ascontiguousarray(
        wo0.reshape(16, 64, D).transpose(1, 0, 2).reshape(64, NH * D))
    args = dict(
        lnv=lnv,
        wq=ktile(f32(inputs["L0_attn_Wq"])),
        wkv=ktile(f32(inputs["L0_attn_Wkv"])),
        wo=wot,
        gwT=ktile(np.ascontiguousarray(f32(inputs["L0_ffn_gate_W"]).T)),
        gbv=f32(inputs["L0_ffn_gate_b"]).reshape(1, NE),
        wib=np.ascontiguousarray(f32(inputs["L0_ffn_wi_b"]).reshape(32, 128).T),
        wob=f32(inputs["L0_ffn_wo_b"]).reshape(1, D),
        wiT=wiTt,
        woT=woTt,
        laT=ktile(np.ascontiguousarray(
            f32(inputs["L0_ffn_lora_A"]).transpose(2, 0, 1).reshape(D, 32))),
        lbT=np.ascontiguousarray(f32(inputs["L0_ffn_lora_B"]).transpose(0, 2, 1).reshape(32, FF)),
        w1=np.ascontiguousarray(np.stack([
            ktile(f32(inputs["L1_attn_W_Q"])), ktile(f32(inputs["L1_attn_W_K"])),
            ktile(f32(inputs["L1_attn_W_Kp"])), ktile(f32(inputs["L1_attn_W_V"])),
            ktile(f32(inputs["L1_attn_W_O"]))])),
        rwT=ktile(np.ascontiguousarray(f32(inputs["L1_ffn_router_weight"]).T)),
        rbv=f32(inputs["L1_ffn_router_bias"]).reshape(1, NE),
        guT=guTt,
        gub=np.ascontiguousarray(f32(inputs["L1_ffn_gate_up_proj_bias"]).reshape(NE, 32, 128).transpose(0, 2, 1)),
        dnT=dnTt,
        dnb=f32(inputs["L1_ffn_down_proj_bias"]).reshape(1, -1),
    )
    return args


def kernel(**inputs):
    x = np.ascontiguousarray(np.asarray(inputs["L0_attn_x"]), dtype=np.float32)
    args = host_prep(inputs)

    if "nc" not in _NC_CACHE:
        _NC_CACHE["nc"] = build_nc()
    nc = _NC_CACHE["nc"]

    in_maps = []
    for c in range(8):
        b, h = c // 2, c % 2
        own = x[b, h * HALF:(h + 1) * HALF]
        oth = x[b, (1 - h) * HALF:(2 - h) * HALF]
        xb = np.ascontiguousarray(np.concatenate([own, oth], axis=0))
        rowflag = np.zeros((128, 1), np.float32)
        if h == 1:
            rowflag[96:128] = 1.0
        m = dict(args)
        m["xb"] = xb
        m["rowflag"] = rowflag
        in_maps.append(m)

    res = run_bass_kernel_spmd(nc, in_maps, core_ids=list(range(8)))
    out = np.empty((4, S, D), np.float32)
    for c in range(8):
        b, h = c // 2, c % 2
        out[b, h * HALF:(h + 1) * HALF] = res.results[c]["out"]
    return out
